# version 01
# baseline (speedup 1.0000x reference)
import numpy as np
import ml_dtypes
import concourse.bass as bass
import concourse.mybir as mybir
from concourse.bass_utils import run_bass_kernel_spmd

F32 = mybir.dt.float32
BF16 = mybir.dt.bfloat16
AF = mybir.ActivationFunctionType
ALU = mybir.AluOpType
BFD = ml_dtypes.bfloat16
NCORE = 8
B, S, D, N, H = 4, 4096, 1024, 128, 512
W = S // NCORE
TOK = B * W
ZD = 1536
BFIX = 64
EPS = 1e-5

_prog_cache = {}


def f32c(a):
    return np.ascontiguousarray(a, dtype=np.float32)


def chunk128(v):
    return np.ascontiguousarray(v.reshape(-1, 128).T, dtype=np.float32)



def ld(nc, name, shape, dtype=F32):
    return nc.dram_tensor(name, list(shape), dtype, kind="ExternalInput").ap()


def st(nc, name, shape, dtype=F32):
    return nc.dram_tensor(name, list(shape), dtype, kind="ExternalOutput").ap()


class Ctx:

    def __init__(self, nc):
        self.nc = nc
        self._sems = []
        self.ld_sem = None
        self.ld_cnt = 0
        self.st_sem = None
        self.st_cnt = 0

    def sem(self, name):
        s = self.nc.semaphore(name).__enter__()
        self._sems.append(s)
        return s

    def init(self):
        self.ld_sem = self.sem("ldsem")
        self.st_sem = self.sem("stsem")

    def dma_in(self, dst, src, cast=False):
        eng = self.nc.gpsimd if cast else self.nc.sync
        eng.dma_start(dst, src).then_inc(self.ld_sem, 16)
        self.ld_cnt += 16

    def wait_in(self):
        self.nc.sync.wait_ge(self.ld_sem, self.ld_cnt)
        self.nc.gpsimd.wait_ge(self.ld_sem, self.ld_cnt)
        self.nc.all_engine_barrier()

    def dma_out(self, dst, src, cast=False):
        eng = self.nc.gpsimd if cast else self.nc.sync
        eng.dma_start(dst, src).then_inc(self.st_sem, 16)
        self.st_cnt += 16

    def finish(self):
        self.nc.all_engine_barrier()
        self.nc.sync.wait_ge(self.st_sem, self.st_cnt)

    def bar(self):
        self.nc.all_engine_barrier()

    def chained(self, eng, name):
        return Chained(self, eng, name)

    def setup_consts(self, tile, values):
        self.cmap = {}
        for j, v in enumerate(values):
            self.nc.gpsimd.memset(tile[:, j:j + 1], float(v))
            self.cmap[float(v)] = tile[:, j:j + 1]

    def const(self, v):
        return self.cmap[float(v)]


class Chained:

    def __init__(self, ctx, eng, name):
        self.eng = eng
        self.sem = ctx.sem(name)
        self.n = 0

    def __getattr__(self, attr):
        f = getattr(self.eng, attr)

        def wrap(*a, **k):
            if self.n:
                self.eng.wait_ge(self.sem, self.n)
            inst = f(*a, **k)
            inst.then_inc(self.sem, 1)
            self.n += 1
            return inst
        return wrap


def new_nc():
    return bass.Bass("TRN2", target_bir_lowering=False, num_devices=NCORE)


def ln_tile(nc, v, sc, out_ap, x_ap, stats, col, eps_ap, gb=None, dd=D):
    mu = stats[:, col + 0:col + 1]
    m2 = stats[:, col + 1:col + 2]
    vr = stats[:, col + 2:col + 3]
    rs = stats[:, col + 3:col + 4]
    v.reduce_sum(mu, x_ap, axis=mybir.AxisListType.X)
    v.tensor_scalar_mul(mu, mu, 1.0 / dd)
    v.tensor_mul(out_ap, x_ap, x_ap)
    v.reduce_sum(m2, out_ap, axis=mybir.AxisListType.X)
    v.tensor_scalar_mul(m2, m2, 1.0 / dd)
    v.tensor_mul(vr, mu, mu)
    v.tensor_sub(vr, m2, vr)
    nc.all_engine_barrier()
    sc.activation(vr, vr, AF.Sqrt, bias=eps_ap)
    nc.all_engine_barrier()
    v.reciprocal(rs, vr)
    v.tensor_scalar(out_ap, x_ap, mu, rs, ALU.subtract, ALU.mult)
    if gb is not None:
        v.tensor_mul(out_ap, out_ap, gb[0])
        v.tensor_add(out_ap, out_ap, gb[1])



def bld_A1():
    nc = new_nc()
    x = ld(nc, "x", (TOK, D))
    g_row = ld(nc, "g_row", (1, D))
    b_row = ld(nc, "b_row", (1, D))
    xn_o = st(nc, "xn", (TOK, D))
    xn2_o = st(nc, "xn2", (TOK, D))
    with (
        nc.sbuf_tensor("xt", [128, D], F32) as xt,
        nc.sbuf_tensor("xn_t", [128, D], F32) as xnt,
        nc.sbuf_tensor("xn2_t", [128, D], F32) as xn2t,
        nc.sbuf_tensor("stats", [128, 8], F32) as stats,
        nc.sbuf_tensor("cst", [128, 2], F32) as cst,
        nc.sbuf_tensor("grow", [1, D], F32) as grow,
        nc.sbuf_tensor("brow", [1, D], F32) as brow,
        nc.sbuf_tensor("ones1", [1, 512], F32) as ones1,
        nc.sbuf_tensor("gbc", [128, D], F32) as gbc,
        nc.sbuf_tensor("bbc", [128, D], F32) as bbc,
        nc.psum_tensor("pbc", [128, 512], F32) as pbc,
    ):
        c = Ctx(nc)
        c.init()
        cv = c.chained(nc.vector, "cv")
        cs = c.chained(nc.scalar, "cs")
        c.setup_consts(cst, [EPS])
        c.dma_in(grow[:, :], g_row)
        c.dma_in(brow[:, :], b_row)
        cv.memset(ones1[:, :], 1.0)
        c.wait_in()
        for half in range(2):
            nc.tensor.matmul(pbc[:, :], ones1[:1, 0:128], grow[:, half * 512:(half + 1) * 512],
                             start=True, stop=True)
            c.bar()
            cv.tensor_copy(gbc[:, half * 512:(half + 1) * 512], pbc[:, :])
            c.bar()
            nc.tensor.matmul(pbc[:, :], ones1[:1, 0:128], brow[:, half * 512:(half + 1) * 512],
                             start=True, stop=True)
            c.bar()
            cv.tensor_copy(bbc[:, half * 512:(half + 1) * 512], pbc[:, :])
            c.bar()
        for i in range(16):
            c.dma_in(xt[:, :], x.rearrange("(i p) d -> i p d", p=128)[i])
            nc.sync.wait_ge(c.ld_sem, c.ld_cnt)
            c.bar()
            ln_tile(nc, cv, cs, xnt[:, :], xt[:, :], stats, 0, c.const(EPS))
            c.bar()
            ln_tile(nc, cv, cs, xn2t[:, :], xnt[:, :], stats, 4, c.const(EPS), gb=(gbc[:, :], bbc[:, :]))
            c.bar()
            c.dma_out(xn_o.rearrange("(i p) d -> i p d", p=128)[i], xnt[:, :])
            c.dma_out(xn2_o.rearrange("(i p) d -> i p d", p=128)[i], xn2t[:, :])
            nc.sync.wait_ge(c.st_sem, c.st_cnt)
            c.bar()
        c.finish()
    return nc




def bld_A2(stage=99):
    import os
    nc = new_nc()
    xn2T = ld(nc, "xn2T", (D, TOK))
    BwT = ld(nc, "BwT", (D, N))
    Bb_row = ld(nc, "Bb_row", (1, N))
    Alog_c = ld(nc, "Alog_c", (N, 1))
    iota_t = ld(nc, "iota_t", (1, W))
    iota_l = ld(nc, "iota_l", (1, W))
    yloc_o = st(nc, "yloc", (N, TOK))
    E_o = st(nc, "E", (N, B))
    aj1_o = st(nc, "aj1", (N, W))
    A512_o = st(nc, "A512", (N, 1))
    with (
        nc.sbuf_tensor("x2b", [128, 8 * TOK], BF16) as x2b,
        nc.sbuf_tensor("bwb", [128, 8 * N], BF16) as bwb,
        nc.sbuf_tensor("bbr", [1, N], BF16) as bbr,
        nc.sbuf_tensor("onesb", [1, 1024], BF16) as onesb,
        nc.sbuf_tensor("onesf", [1, 512], F32) as onesf,
        nc.sbuf_tensor("ball", [128, TOK], F32) as ball,
        nc.sbuf_tensor("ybuf", [128, TOK], F32) as ybuf,
        nc.sbuf_tensor("alog", [128, 1], F32) as alog,
        nc.sbuf_tensor("sc1", [128, 16], F32) as sc1,
        nc.sbuf_tensor("itrow", [1, 2 * W], F32) as itrow,
        nc.sbuf_tensor("tblt", [128, W], F32) as tblt,
        nc.sbuf_tensor("apow", [128, W], F32) as apow,
        nc.sbuf_tensor("dg", [128, W], F32) as dg,
        nc.sbuf_tensor("aj1s", [128, W], F32) as aj1,
        nc.psum_tensor("pb", [128, 512], F32) as pb,
        nc.psum_tensor("pb2", [128, 512], F32) as pb2,
    ):
        c = Ctx(nc)
        c.init()
        cv = c.chained(nc.vector, "cv")
        cs = c.chained(nc.scalar, "cs")
        pe = c.sem("pe")
        pec = [0]

        def pemark(inst):
            inst.then_inc(pe, 1)
            pec[0] += 1

        c.dma_in(x2b[:, :].rearrange("p (c t) -> p c t", c=8),
                 xn2T.rearrange("(c p) t -> p c t", p=128), cast=True)
        c.dma_in(bwb[:, :].rearrange("p (c n) -> p c n", c=8),
                 BwT.rearrange("(c p) n -> p c n", p=128), cast=True)
        c.dma_in(bbr[:, :], Bb_row, cast=True)
        c.dma_in(alog[:, :], Alog_c)
        c.dma_in(itrow[:, 0:W], iota_t)
        c.dma_in(itrow[:, W:2 * W], iota_l)
        cv.memset(onesb[:, :], 1.0)
        cv.memset(onesf[:, :], 1.0)
        c.wait_in()
        if stage <= 0:
            c.dma_out(E_o, dg[:, 0:B])
            c.finish()
            return nc
        x2v = x2b[:, :].rearrange("p (c t) -> p c t", c=8)
        bwv = bwb[:, :].rearrange("p (c n) -> p c n", c=8)
        for q in range(int(os.environ.get("A2_NQ", "4"))):
            for ch in range(int(os.environ.get("A2_NCH", "8"))):
                nc.tensor.matmul(pb[:, :], bwv[:, ch, :], x2v[:, ch, q * 512:(q + 1) * 512],
                                 start=(ch == 0), stop=False)
            if os.environ.get("A2_NOBIAS"):
                i = nc.tensor.matmul(pb[:, :], bwv[:, 0, :], x2v[:, 0, q * 512:(q + 1) * 512],
                                     start=False, stop=True)
            else:
                i = nc.tensor.matmul(pb[:, :], bbr[:1, :], onesb[:1, 0:512],
                                     start=False, stop=True)
            pemark(i)
            nc.vector.wait_ge(pe, pec[0])
            cv.tensor_copy(ball[:, q * 512:(q + 1) * 512], pb[:, :])
            nc.tensor.wait_ge(cv.sem, cv.n)
        c.bar()
        if stage <= 1:
            c.dma_out(yloc_o, ball[:, :])
            c.finish()
            return nc
        cs.activation(sc1[:, 0:1], alog[:, :], AF.Sigmoid)
        c.bar()
        cs.activation(sc1[:, 1:2], sc1[:, 0:1], AF.Ln)
        c.bar()
        for k in range(9):
            cs.activation(sc1[:, 2 + k:3 + k], sc1[:, 1:2], AF.Exp, scale=float(1 << k))
        cs.activation(sc1[:, 11:12], sc1[:, 1:2], AF.Exp, scale=float(W))
        c.bar()
        i = nc.tensor.matmul(pb2[:, :], onesf[:1, 0:128], itrow[:, 0:512], start=True, stop=True)
        pemark(i)
        nc.vector.wait_ge(pe, pec[0])
        cv.tensor_copy(tblt[:, :], pb2[:, :])
        nc.tensor.wait_ge(cv.sem, cv.n)
        i = nc.tensor.matmul(pb2[:, :], onesf[:1, 0:128], itrow[:, 512:1024], start=True, stop=True)
        pemark(i)
        nc.vector.wait_ge(pe, pec[0])
        cv.tensor_copy(aj1[:, :], pb2[:, :])
        c.bar()
        cv.tensor_scalar_mul(tblt[:, :], tblt[:, :], sc1[:, 1:2])
        c.bar()
        cs.activation(apow[:, :], tblt[:, :], AF.Exp)
        c.bar()
        cv.tensor_scalar_max(dg[:, :], apow[:, :], 1e-8)
        cv.reciprocal(dg[:, :], dg[:, :])
        cv.tensor_mul(dg[:, :], dg[:, :], apow[:, :])
        cv.tensor_scalar(aj1[:, :], aj1[:, :], 1.0, sc1[:, 1:2], ALU.add, ALU.mult)
        c.bar()
        cs.activation(aj1[:, :], aj1[:, :], AF.Exp)
        c.bar()
        if stage <= 2:
            c.dma_out(yloc_o, ball[:, :])
            c.finish()
            return nc
        for b in range(B):
            cv.tensor_mul(ball[:, b * W:(b + 1) * W], ball[:, b * W:(b + 1) * W], dg[:, :])
        c.bar()
        if stage <= 3:
            c.dma_out(yloc_o, ball[:, :])
            c.finish()
            return nc
        cur, oth = ball, ybuf
        for k in range(9):
            s = 1 << k
            ak = sc1[:, 2 + k:3 + k]
            for b in range(B):
                cv.tensor_copy(oth[:, b * W:b * W + s], cur[:, b * W:b * W + s])
                cv.scalar_tensor_tensor(oth[:, b * W + s:(b + 1) * W],
                                        cur[:, b * W:(b + 1) * W - s], ak,
                                        cur[:, b * W + s:(b + 1) * W],
                                        ALU.mult, ALU.add)
            cur, oth = oth, cur
        c.bar()
        ecol = cur[:, :].rearrange("p (b t) -> p b t", b=B)
        cv.tensor_copy(dg[:, 0:B], ecol[:, :, W - 1])
        c.bar()
        c.dma_out(yloc_o, cur[:, :])
        c.dma_out(E_o, dg[:, 0:B])
        c.dma_out(aj1_o, aj1[:, :])
        c.dma_out(A512_o, sc1[:, 11:12])
        c.finish()
    return nc



def bld_A3():
    nc = new_nc()
    E_all = ld(nc, "E_all", (N, NCORE * B))
    A512 = ld(nc, "A512", (N, 1))
    S_o = st(nc, "S_all", (N, NCORE * B))
    with (
        nc.sbuf_tensor("e", [128, NCORE * B], F32) as e,
        nc.sbuf_tensor("s", [128, NCORE * B], F32) as s,
        nc.sbuf_tensor("a5", [128, 1], F32) as a5,
    ):
        c = Ctx(nc)
        c.init()
        cv = c.chained(nc.vector, "cv")
        c.dma_in(e[:, :], E_all)
        c.dma_in(a5[:, :], A512)
        c.wait_in()
        cv.tensor_copy(s[:, 0:B], e[:, 0:B])
        for r in range(1, NCORE):
            cv.scalar_tensor_tensor(s[:, r * B:(r + 1) * B], s[:, (r - 1) * B:r * B],
                                    a5[:, :], e[:, r * B:(r + 1) * B], ALU.mult, ALU.add)
        c.bar()
        c.dma_out(S_o, s[:, :])
        c.finish()
    return nc



def bld_A4():
    nc = new_nc()
    yloc = ld(nc, "yloc", (N, TOK))
    aj1 = ld(nc, "aj1", (N, W))
    carry = ld(nc, "carry", (N, B))
    Cw = ld(nc, "Cw", (D, N))
    CbT = ld(nc, "CbT", (128, 8))
    ff1T = ld(nc, "ff1T", (D + H, H))
    ff2T = ld(nc, "ff2T", (D + H, H))
    taT = ld(nc, "taT", (D + H, H))
    tbT = ld(nc, "tbT", (D + H, H))
    fb = ld(nc, "fb", (1, 4 * H))
    P_o = st(nc, "P", (128, 12 * TOK), BF16)
    with (
        nc.sbuf_tensor("y", [128, TOK], F32) as y,
        nc.sbuf_tensor("aj", [128, W], F32) as aj,
        nc.sbuf_tensor("car", [128, B], F32) as car,
        nc.sbuf_tensor("stb", [128, TOK], BF16) as stb,
        nc.sbuf_tensor("cwb", [128, 8 * N], BF16) as cwb,
        nc.sbuf_tensor("cbt", [128, 8], BF16) as cbt,
        nc.sbuf_tensor("wxt", [128, 8 * ZD], BF16) as wxt,
        nc.sbuf_tensor("tas", [128, 8 * H], F32) as tas,
        nc.sbuf_tensor("tbs", [128, 8 * H], F32) as tbs,
        nc.sbuf_tensor("wcomb", [128, ZD], BF16) as wcomb,
        nc.sbuf_tensor("fbs", [1, 4 * H], F32) as fbs,
        nc.sbuf_tensor("ball", [1, ZD], F32) as ballr,
        nc.sbuf_tensor("bcomb", [1, ZD], BF16) as bcomb,
        nc.sbuf_tensor("onesb", [1, 1024], BF16) as onesb,
        nc.psum_tensor("pw", [128, 512], F32) as pw,
        nc.psum_tensor("pw2", [128, 512], F32) as pw2,
        nc.psum_tensor("pbias", [1, 512], F32) as pbias,
    ):
        c = Ctx(nc)
        c.init()
        cv = c.chained(nc.vector, "cv")
        pe = c.sem("pe")
        pec = [0]

        def pemark(inst):
            inst.then_inc(pe, 1)
            pec[0] += 1

        c.dma_in(y[:, :], yloc)
        c.dma_in(aj[:, :], aj1)
        c.dma_in(car[:, :], carry)
        c.dma_in(cwb[:, :].rearrange("p (c n) -> p c n", c=8),
                 Cw.rearrange("(c p) n -> p c n", p=128), cast=True)
        c.dma_in(cbt[:, :], CbT, cast=True)
        wxv = wxt[:, :].rearrange("p (c z) -> p c z", c=8)
        c.dma_in(wxv[:, :, 0:512], ff1T.rearrange("(c p) h -> p c h", p=128)[:, 0:8], cast=True)
        c.dma_in(wxv[:, :, 512:1024], ff2T.rearrange("(c p) h -> p c h", p=128)[:, 0:8], cast=True)
        c.dma_in(tas[:, :].rearrange("p (c h) -> p c h", c=8),
                 taT.rearrange("(c p) h -> p c h", p=128)[:, 0:8])
        c.dma_in(tbs[:, :].rearrange("p (c h) -> p c h", c=8),
                 tbT.rearrange("(c p) h -> p c h", p=128)[:, 0:8])
        c.dma_in(fbs[:, :], fb)
        cv.memset(onesb[:, :], 1.0)
        c.wait_in()
        for b in range(B):
            cv.scalar_tensor_tensor(y[:, b * W:(b + 1) * W], aj[:, :], car[:, b:b + 1],
                                    y[:, b * W:(b + 1) * W], ALU.mult, ALU.add)
        cv.tensor_copy(stb[:, :], y[:, :])
        cv.scalar_tensor_tensor(wxv[:, :, 1024:1536],
                                tas[:, :].rearrange("p (c h) -> p c h", c=8), 0.1,
                                tbs[:, :].rearrange("p (c h) -> p c h", c=8), ALU.mult, ALU.add)
        fbv = fbs[:, :].rearrange("o (r h) -> o r h", r=4)
        cv.tensor_copy(ballr[:, 0:512], fbv[:, 0, :])
        cv.tensor_copy(ballr[:, 512:1024], fbv[:, 1, :])
        cv.scalar_tensor_tensor(ballr[:, 1024:1536], fbv[:, 2, :], 0.1, fbv[:, 3, :],
                                ALU.mult, ALU.add)
        c.bar()
        cwv = cwb[:, :].rearrange("p (c n) -> p c n", c=8)
        for seg in range(3):
            z0 = seg * 512
            for chv in range(8):
                i = nc.tensor.matmul(pw2[:, :], cwv[:, chv, :],
                                     wxv[:, chv, z0:z0 + 512], start=(chv == 0), stop=(chv == 7))
            pemark(i)
            nc.vector.wait_ge(pe, pec[0])
            cv.tensor_copy(wcomb[:, z0:z0 + 512], pw2[:, :])
            nc.tensor.wait_ge(cv.sem, cv.n)
        c.bar()
        for seg in range(3):
            z0 = seg * 512
            for chv in range(8):
                i = nc.tensor.matmul(pbias[:, :], cbt[:, chv:chv + 1], wxv[:, chv, z0:z0 + 512],
                                     start=(chv == 0), stop=(chv == 7))
            pemark(i)
            nc.vector.wait_ge(pe, pec[0])
            cv.tensor_add(ballr[:, z0:z0 + 512], ballr[:, z0:z0 + 512], pbias[:1, :])
            nc.tensor.wait_ge(cv.sem, cv.n)
        cv.tensor_copy(bcomb[:, :], ballr[:, :])
        c.bar()
        pov = P_o.rearrange("p (zt t) -> p zt t", zt=12)
        with nc.sbuf_tensor("pout", [128, 2048], BF16) as pout:
            for zt in range(12):
                for q in range(4):
                    nc.tensor.matmul(pw[:, :], wcomb[:, zt * 128:(zt + 1) * 128],
                                     stb[:, q * 512:(q + 1) * 512], start=True, stop=False)
                    i = nc.tensor.matmul(pw[:, :], bcomb[:1, zt * 128:(zt + 1) * 128],
                                         onesb[:1, 0:512], start=False, stop=True)
                    pemark(i)
                    nc.vector.wait_ge(pe, pec[0])
                    cv.tensor_copy(pout[:, q * 512:(q + 1) * 512], pw[:, :])
                    nc.tensor.wait_ge(cv.sem, cv.n)
                c.bar()
                c.dma_out(pov[:, zt, :], pout[:, :])
                nc.sync.wait_ge(c.st_sem, c.st_cnt)
                c.bar()
        c.finish()
    return nc



def emit_scan(nc, c, cv_unused, P_sb, whv, cfc, pz, zs, gg, tmp, sP, sV, sA, T):
    import os
    parts = os.environ.get("SCAN_PARTS", "all")
    pv = P_sb.rearrange("p (zt b t) -> p zt b t", zt=12, b=B)
    te, ve, se = nc.tensor, nc.vector, nc.scalar
    for t in range(T):
        pzt = pz[t % 2]
        zst = zs[t % 2]
        ggt = gg[t % 2]
        pzv = pzt[:, :].rearrange("p (zt b) -> p zt b", zt=12)
        if t > 0:
            te.wait_ge(sV, 4 * t)
        nacc = int(os.environ.get("SCAN_NACC", "4"))
        for zt in range(12):
            for hc in range(nacc):
                i = te.matmul(pzt[:, zt * 4:(zt + 1) * 4],
                              whv[:, hc, zt * 128:(zt + 1) * 128],
                              cfc[:, t, hc * 4:(hc + 1) * 4],
                              start=(hc == 0), stop=(hc == nacc - 1))
        i.then_inc(sP, 1)
        if parts == "pe":
            ve.wait_ge(sP, t + 1)
            ve.tensor_tensor(cfc[:, t + 1, :], cfc[:, t, :], cfc[:, t, :], ALU.add).then_inc(sV, 4)
            se.nop().then_inc(sA, 2)
            continue
        ve.wait_ge(sP, t + 1)
        if t >= 2:
            ve.wait_ge(sA, 2 * (t - 1))
        ve.tensor_tensor(zst[:, :].rearrange("p (zt b) -> p zt b", zt=12),
                         pzv, pv[:, :, :, t], ALU.add).then_inc(sV, 1)
        if parts == "pedve":
            ve.tensor_copy(cfc[:, t + 1, :], zst[:, 0:16]).then_inc(sV, 3)
            se.nop().then_inc(sA, 2)
            continue
        se.wait_ge(sV, 4 * t + 1)
        af2 = AF.Tanh if os.environ.get("SCAN_TANHONLY") else AF.Sigmoid
        se.activation(ggt[:, 0:32], zst[:, 0:32], AF.Tanh).then_inc(sA, 1)
        se.activation(ggt[:, 32:48], zst[:, 32:48], af2).then_inc(sA, 1)
        ve.wait_ge(sA, 2 * (t + 1))
        ve.tensor_sub(tmp[:, 0:16], ggt[:, 16:32], ggt[:, 0:16]).then_inc(sV, 1)
        ve.wait_ge(sV, 4 * t + 2)
        ve.tensor_mul(tmp[:, 16:32], ggt[:, 32:48], tmp[:, 0:16]).then_inc(sV, 1)
        ve.wait_ge(sV, 4 * t + 3)
        ve.tensor_add(cfc[:, t + 1, :], ggt[:, 0:16], tmp[:, 16:32]).then_inc(sV, 1)


def bld_scan(T, with_carry, stage=99):
    import os
    nc = new_nc()
    P_in = ld(nc, "P", (128, 12 * B * T), BF16)
    ff1T = ld(nc, "ff1T", (D + H, H))
    ff2T = ld(nc, "ff2T", (D + H, H))
    taT = ld(nc, "taT", (D + H, H))
    tbT = ld(nc, "tbT", (D + H, H))
    if with_carry:
        carry = ld(nc, "carry", (128, 16))
    cfc_o = st(nc, "cfc", (128, T * 16), BF16)
    hend_o = st(nc, "hend", (128, 16))
    with (
        nc.sbuf_tensor("psb", [128, 12 * B * T], BF16) as psb,
        nc.sbuf_tensor("wht", [128, 4 * ZD], BF16) as wht,
        nc.sbuf_tensor("tas", [128, 4 * H], F32) as tas,
        nc.sbuf_tensor("tbs", [128, 4 * H], F32) as tbs,
        nc.sbuf_tensor("cfcs", [128, (T + 1) * 16], BF16) as cfcs,
        nc.sbuf_tensor("zsA", [128, 48], F32) as zsA,
        nc.sbuf_tensor("zsB", [128, 48], F32) as zsB,
        nc.sbuf_tensor("ggA", [128, 48], F32) as ggA,
        nc.sbuf_tensor("ggB", [128, 48], F32) as ggB,
        nc.sbuf_tensor("tmp", [128, 32], F32) as tmp,
        nc.sbuf_tensor("hef", [128, 16], F32) as hef,
        nc.psum_tensor("pzA", [128, 48], F32) as pzA,
        nc.psum_tensor("pzB", [128, 48], F32) as pzB,
    ):
        c = Ctx(nc)
        c.init()
        cv = c.chained(nc.vector, "cv")
        sP = c.sem("sP")
        sV = c.sem("sV")
        sA = c.sem("sA")
        c.dma_in(psb[:, :], P_in)
        whv = wht[:, :].rearrange("p (hc z) -> p hc z", hc=4)
        c.dma_in(whv[:, :, 0:512], ff1T.rearrange("(c p) h -> p c h", p=128)[:, 8:12], cast=True)
        c.dma_in(whv[:, :, 512:1024], ff2T.rearrange("(c p) h -> p c h", p=128)[:, 8:12], cast=True)
        c.dma_in(tas[:, :].rearrange("p (c h) -> p c h", c=4),
                 taT.rearrange("(c p) h -> p c h", p=128)[:, 8:12])
        c.dma_in(tbs[:, :].rearrange("p (c h) -> p c h", c=4),
                 tbT.rearrange("(c p) h -> p c h", p=128)[:, 8:12])
        if with_carry:
            c.dma_in(hef[:, :], carry)
        nc.scalar.preload_act_table(AF.Sigmoid)
        c.wait_in()
        if stage <= 0:
            cv.tensor_copy(hef[:, :], tas[:, 0:16])
            c.bar()
            c.dma_out(cfc_o, cfcs[:, 16:])
            c.dma_out(hend_o, hef[:, :])
            c.finish()
            return nc
        cv.scalar_tensor_tensor(whv[:, :, 1024:1536],
                                tas[:, :].rearrange("p (c h) -> p c h", c=4), 0.1,
                                tbs[:, :].rearrange("p (c h) -> p c h", c=4), ALU.mult, ALU.add)
        cfcv = cfcs[:, :].rearrange("p (t h) -> p t h", h=16)
        if with_carry:
            cv.tensor_copy(cfcv[:, 0, :], hef[:, :])
        else:
            cv.memset(cfcv[:, 0, :], 0.0)
        c.bar()
        if stage <= 1:
            cv.tensor_copy(hef[:, :], tas[:, 0:16])
            c.bar()
            c.dma_out(cfc_o, cfcs[:, 16:])
            c.dma_out(hend_o, hef[:, :])
            c.finish()
            return nc
        emit_scan(nc, c, cv, psb[:, :], whv, cfcv, (pzA, pzB), (zsA, zsB), (ggA, ggB),
                  tmp, sP, sV, sA, T)
        c.bar()
        cv.tensor_copy(hef[:, :], cfcv[:, T, :])
        c.bar()
        c.dma_out(cfc_o, cfcs[:, 16:])
        c.dma_out(hend_o, hef[:, :])
        c.finish()
    return nc


def bld_A5():
    return bld_scan(W, False)


def bld_A6():
    return bld_scan(BFIX, True)



def bld_A7(stage=99):
    import os
    nc = new_nc()
    cfc = ld(nc, "cfc", (128, W * 16), BF16)
    owT = ld(nc, "owT", (H, D))
    ob_row = ld(nc, "ob_row", (1, D))
    xn = ld(nc, "xn", (TOK, D))
    xnT = ld(nc, "xnT", (D, TOK))
    rw1 = ld(nc, "rw1", (D, D))
    rb1_row = ld(nc, "rb1_row", (1, D))
    rw2 = ld(nc, "rw2", (D, 8))
    rb2_row = ld(nc, "rb2_row", (1, 8))
    s0_o = st(nc, "s0", (TOK, D))
    rlog_o = st(nc, "rlog", (8, TOK))
    with (
        nc.sbuf_tensor("cfcs", [128, W * 16], BF16) as cfcs,
        nc.sbuf_tensor("owb", [128, 4 * D], BF16) as owb,
        nc.sbuf_tensor("obr", [1, D], BF16) as obr,
        nc.sbuf_tensor("xntb", [128, 16 * D], BF16) as xntb,
        nc.sbuf_tensor("xnTb", [128, 8 * TOK], BF16) as xnTb,
        nc.sbuf_tensor("rw1b", [128, 8 * D], BF16) as rw1b,
        nc.sbuf_tensor("rb1b", [1, D], BF16) as rb1b,
        nc.sbuf_tensor("rw2b", [128, 8 * 8], BF16) as rw2b,
        nc.sbuf_tensor("rb2b", [1, 8], BF16) as rb2b,
        nc.sbuf_tensor("onesb", [1, 512], BF16) as onesb,
        nc.sbuf_tensor("s0t", [128, D], F32) as s0t,
        nc.sbuf_tensor("lts", [128, 8 * TOK], BF16) as lts,
        nc.sbuf_tensor("rls", [8, TOK], F32) as rls,
        nc.psum_tensor("ps", [128, 512], F32) as ps,
        nc.psum_tensor("pl", [8, 512], F32) as pl,
    ):
        c = Ctx(nc)
        c.init()
        cv = c.chained(nc.vector, "cv")
        cs = c.chained(nc.scalar, "cs")
        pe = c.sem("pe")
        pec = [0]

        def pemark(i):
            i.then_inc(pe, 1)
            pec[0] += 1

        c.dma_in(cfcs[:, :], cfc)
        c.dma_in(owb[:, :].rearrange("p (c d) -> p c d", c=4),
                 owT.rearrange("(c p) d -> p c d", p=128), cast=True)
        c.dma_in(obr[:, :], ob_row, cast=True)
        c.dma_in(xntb[:, :].rearrange("p (i d) -> p i d", i=16),
                 xn.rearrange("(i p) d -> p i d", p=128), cast=True)
        c.dma_in(xnTb[:, :].rearrange("p (c t) -> p c t", c=8),
                 xnT.rearrange("(c p) t -> p c t", p=128), cast=True)
        c.dma_in(rw1b[:, :].rearrange("p (c f) -> p c f", c=8),
                 rw1.rearrange("(c p) f -> p c f", p=128), cast=True)
        c.dma_in(rb1b[:, :], rb1_row, cast=True)
        c.dma_in(rw2b[:, :].rearrange("p (c e) -> p c e", c=8),
                 rw2.rearrange("(c p) e -> p c e", p=128), cast=True)
        c.dma_in(rb2b[:, :], rb2_row, cast=True)
        cv.memset(onesb[:, :], 1.0)
        c.wait_in()
        cfcv = cfcs[:, :].rearrange("p (t h) -> p t h", h=16)
        owv = owb[:, :].rearrange("p (c d) -> p c d", c=4)
        xnv = xntb[:, :].rearrange("p (i d) -> p i d", i=16)
        for i in range(16):
            b = i // 4
            t0 = (i % 4) * 128
            for q in range(2):
                for hc in range(4):
                    nc.tensor.matmul(ps[:, :], cfcv[:, t0:t0 + 128, hc * 4 + b],
                                     owv[:, hc, q * 512:(q + 1) * 512],
                                     start=(hc == 0), stop=False)
                ii = nc.tensor.matmul(ps[:, :], onesb[:1, 0:128], obr[:1, q * 512:(q + 1) * 512],
                                      start=False, stop=True)
                pemark(ii)
                nc.vector.wait_ge(pe, pec[0])
                cv.tensor_tensor(s0t[:, q * 512:(q + 1) * 512], ps[:, :],
                                 xnv[:, i, q * 512:(q + 1) * 512], ALU.add)
                nc.tensor.wait_ge(cv.sem, cv.n)
            c.bar()
            c.dma_out(s0_o.rearrange("(i p) d -> p i d", p=128)[:, i], s0t[:, :])
            nc.sync.wait_ge(c.st_sem, c.st_cnt)
            c.bar()
        c.bar()
        if stage <= 1:
            c.dma_out(rlog_o, rls[:, :])
            c.finish()
            return nc
        xtv = xnTb[:, :].rearrange("p (c t) -> p c t", c=8)
        rv1 = rw1b[:, :].rearrange("p (c f) -> p c f", c=8)
        ltv = lts[:, :].rearrange("p (f t) -> p f t", f=8)
        for ft in range(8):
            for q in range(4):
                for ch in range(8):
                    nc.tensor.matmul(ps[:, :], rv1[:, ch, ft * 128:(ft + 1) * 128],
                                     xtv[:, ch, q * 512:(q + 1) * 512],
                                     start=(ch == 0), stop=False)
                ii = nc.tensor.matmul(ps[:, :], rb1b[:1, ft * 128:(ft + 1) * 128],
                                      onesb[:1, :], start=False, stop=True)
                pemark(ii)
                nc.scalar.wait_ge(pe, pec[0])
                cs.activation(ltv[:, ft, q * 512:(q + 1) * 512], ps[:, :], AF.Silu)
                nc.tensor.wait_ge(cs.sem, cs.n)
        c.bar()
        if stage <= 2:
            c.dma_out(rlog_o, rls[:, :])
            c.finish()
            return nc
        rv2 = rw2b[:, :].rearrange("p (c e) -> p c e", c=8)
        for q in range(4):
            for ch in range(8):
                nc.tensor.matmul(pl[:, :], rv2[:, ch, :], ltv[:, ch, q * 512:(q + 1) * 512],
                                 start=(ch == 0), stop=False)
            ii = nc.tensor.matmul(pl[:, :], rb2b[:1, :], onesb[:1, :],
                                  start=False, stop=True)
            pemark(ii)
            nc.vector.wait_ge(pe, pec[0])
            cv.tensor_copy(rls[:, q * 512:(q + 1) * 512], pl[:, :])
            nc.tensor.wait_ge(cv.sem, cv.n)
        c.bar()
        c.dma_out(rlog_o, rls[:, :])
        c.finish()
    return nc



def bld_A8a():
    nc = new_nc()
    rlt = ld(nc, "rlt", (TOK, 8))
    w_o = st(nc, "w", (TOK, 8))
    with (
        nc.sbuf_tensor("lg", [128, 16 * 8], F32) as lg,
        nc.sbuf_tensor("wv", [128, 16 * 8], F32) as wv,
        nc.sbuf_tensor("sc", [128, 8], F32) as scr,
        nc.sbuf_tensor("negb", [128, 8], F32) as negb,
        nc.sbuf_tensor("eqv", [128, 8], F32) as eqv,
        nc.sbuf_tensor("ev", [128, 8], F32) as ev,
    ):
        c = Ctx(nc)
        c.init()
        cv = c.chained(nc.vector, "cv")
        cs = c.chained(nc.scalar, "cs")
        c.dma_in(lg[:, :].rearrange("p (i e) -> p i e", i=16),
                 rlt.rearrange("(i p) e -> p i e", p=128))
        nc.gpsimd.memset(negb[:, :], -1e30)
        c.wait_in()
        lgv = lg[:, :].rearrange("p (i e) -> p i e", i=16)
        wvv = wv[:, :].rearrange("p (i e) -> p i e", i=16)
        for i in range(16):
            t = lgv[:, i, :]
            cv.reduce_max(scr[:, 0:1], t, axis=mybir.AxisListType.X)
            cv.tensor_scalar(eqv[:, :], t, scr[:, 0:1], None, ALU.is_equal)
            cv.scalar_tensor_tensor(ev[:, :], eqv[:, :], -1e30, t, ALU.mult, ALU.add)
            cv.reduce_max(scr[:, 1:2], ev[:, :], axis=mybir.AxisListType.X)
            cv.tensor_scalar_mul(scr[:, 2:3], scr[:, 0:1], -1.0)
            c.bar()
            cs.activation(ev[:, :], t, AF.Exp, bias=scr[:, 2:3])
            c.bar()
            cv.tensor_scalar(eqv[:, :], t, scr[:, 1:2], None, ALU.is_ge)
            cv.tensor_mul(ev[:, :], ev[:, :], eqv[:, :])
            cv.reduce_sum(scr[:, 3:4], ev[:, :], axis=mybir.AxisListType.X)
            cv.reciprocal(scr[:, 4:5], scr[:, 3:4])
            cv.tensor_scalar_mul(wvv[:, i, :], ev[:, :], scr[:, 4:5])
        c.bar()
        c.dma_out(w_o.rearrange("(i p) e -> p i e", p=128), wvv)
        c.finish()
    return nc



def bld_A8b():
    nc = new_nc()
    wsl = ld(nc, "wsl", (8, TOK))
    xnT = ld(nc, "xnT", (D, TOK))
    ew1 = ld(nc, "ew1", (8 * D, 2048))
    eb1 = ld(nc, "eb1", (8, 2048))
    ew2 = ld(nc, "ew2", (8 * 2048, D))
    eb2 = ld(nc, "eb2", (8, D))
    uc = ld(nc, "uc", (8, 8 * 128))
    g0T_o = st(nc, "g0T", (D, TOK), BF16)
    from contextlib import ExitStack
    with ExitStack() as _es:
        def _sb(nm, shp, dt=F32):
            return _es.enter_context(nc.sbuf_tensor(nm, shp, dt))
        wslb = _sb("wslb", [8, TOK], BF16)
        xnTb = _sb("xnTb", [128, 8 * TOK], BF16)
        eb2b = _sb("eb2b", [8, D], BF16)
        ucb = _sb("ucb", [8, 8 * 128], BF16)
        onesb = _sb("onesb", [1, 512], BF16)
        eb1r = _sb("eb1r", [1, 2048], BF16)
        yT = _sb("yT", [128, 8 * TOK], F32)
        ehw = _sb("ehw", [128, 16 * 1024], BF16)
        wbc = _sb("wbc", [128, 1024], BF16)
        w1b0 = _sb("w1b0", [128, 8 * 128], BF16)
        w1b1 = _sb("w1b1", [128, 8 * 128], BF16)
        w2b0 = _sb("w2b0", [128, 16 * 128], BF16)
        w2b1 = _sb("w2b1", [128, 16 * 128], BF16)
        ehc = _sb("ehc", [128, 1024], BF16)
        g0t = _sb("g0t", [128, 1024], F32)
        pa = _es.enter_context(nc.psum_tensor("pa", [128, 512], F32))
        pb = _es.enter_context(nc.psum_tensor("pb", [128, 512], F32))
        c = Ctx(nc)
        c.init()
        cv = c.chained(nc.vector, "cv")
        cs = c.chained(nc.scalar, "cs")
        pe = c.sem("pe")
        pec = [0]
        wsem = c.sem("wsem")
        wcnt = [0]

        def pemark(i):
            i.then_inc(pe, 1)
            pec[0] += 1

        c.dma_in(wslb[:, :], wsl, cast=True)
        c.dma_in(xnTb[:, :].rearrange("p (c t) -> p c t", c=8),
                 xnT.rearrange("(c p) t -> p c t", p=128), cast=True)
        c.dma_in(eb2b[:, :], eb2, cast=True)
        c.dma_in(ucb[:, :], uc, cast=True)
        cv.memset(onesb[:, :], 1.0)
        cv.memset(yT[:, :], 0.0)
        c.wait_in()
        xtv = xnTb[:, :].rearrange("p (c t) -> p c t", c=8)
        ehv = ehw[:, :].rearrange("p (f t) -> p f t", f=16)
        w1v = [w1b0[:, :].rearrange("p (c f) -> p c f", c=8),
               w1b1[:, :].rearrange("p (c f) -> p c f", c=8)]
        w2v = [w2b0[:, :].rearrange("p (f d) -> p f d", f=16),
               w2b1[:, :].rearrange("p (f d) -> p f d", f=16)]
        yv = yT[:, :].rearrange("p (d t) -> p d t", d=8)
        ew1v = ew1.rearrange("(e c p) f -> p e c f", e=8, p=128)
        ew2v = ew2.rearrange("(e f p) d -> p e f d", e=8, p=128)
        import os
        NSLAB = int(os.environ.get("A8B_NSLAB", "2"))
        NEXP = int(os.environ.get("A8B_NEXP", "8"))
        for slab in range(NSLAB):
            toff = slab * 1024
            for e in range(NEXP):
                for q in range(2):
                    ii = nc.tensor.matmul(pa[:, :], ucb[:, e * 128:(e + 1) * 128],
                                          wslb[:, toff + q * 512:toff + (q + 1) * 512],
                                          start=True, stop=True)
                    pemark(ii)
                    nc.vector.wait_ge(pe, pec[0])
                    cv.tensor_copy(wbc[:, q * 512:(q + 1) * 512], pa[:, :])
                    nc.tensor.wait_ge(cv.sem, cv.n)
                nc.gpsimd.wait_ge(pe, pec[0])
                c.dma_in(eb1r[:, :], eb1[e:e + 1, :], cast=True)
                nc.sync.wait_ge(c.ld_sem, c.ld_cnt)
                c.bar()
                for fc in range(16):
                    wb = w1v[0 if os.environ.get("A8B_1BUF") else fc % 2]
                    nc.gpsimd.wait_ge(pe, pec[0])
                    c.dma_in(wb, ew1v[:, e, :, fc * 128:(fc + 1) * 128], cast=True)
                    nc.sync.wait_ge(c.ld_sem, c.ld_cnt)
                    nc.gpsimd.wait_ge(c.ld_sem, c.ld_cnt)
                    c.bar()
                    for q in range(2):
                        for dc in range(8):
                            nc.tensor.matmul(pb[:, :], wb[:, dc, :],
                                             xtv[:, dc, toff + q * 512:toff + (q + 1) * 512],
                                             start=(dc == 0), stop=False)
                        ii = nc.tensor.matmul(pb[:, :], eb1r[:1, fc * 128:(fc + 1) * 128],
                                              onesb[:1, :], start=False, stop=True)
                        pemark(ii)
                        nc.scalar.wait_ge(pe, pec[0])
                        nc.scalar.wait_ge(cv.sem, cv.n)
                        cs.activation(ehc[:, q * 512:(q + 1) * 512], pb[:, :], AF.Silu)
                        nc.tensor.wait_ge(cs.sem, cs.n)
                    nc.vector.wait_ge(cs.sem, cs.n)
                    if os.environ.get("A8B_WONE"):
                        cv.tensor_copy(ehv[:, fc, :], ehc[:, :])
                    else:
                        cv.tensor_mul(ehv[:, fc, :], ehc[:, :], wbc[:, :])
                c.bar()
                for dt in range(8):
                    wb2 = w2v[dt % 2]
                    nc.gpsimd.wait_ge(pe, pec[0])
                    c.dma_in(wb2, ew2v[:, e, :, dt * 128:(dt + 1) * 128], cast=True)
                    nc.sync.wait_ge(c.ld_sem, c.ld_cnt)
                    nc.gpsimd.wait_ge(c.ld_sem, c.ld_cnt)
                    c.bar()
                    for q in range(2):
                        for fc in range(16):
                            ii = nc.tensor.matmul(pa[:, :], w2v[dt % 2][:, fc, :],
                                                  ehv[:, fc, q * 512:(q + 1) * 512],
                                                  start=(fc == 0), stop=(fc == 15))
                        pemark(ii)
                        nc.vector.wait_ge(pe, pec[0])
                        cv.tensor_tensor(yv[:, dt, toff + q * 512:toff + (q + 1) * 512],
                                         pa[:, :],
                                         yv[:, dt, toff + q * 512:toff + (q + 1) * 512],
                                         ALU.add)
                        nc.tensor.wait_ge(cv.sem, cv.n)
                c.bar()
        if os.environ.get("A8B_DUMPEH"):
            for fc in range(8):
                cv.tensor_copy(g0t[:, :], ehv[:, fc, :])
                c.bar()
                c.dma_out(g0T_o.rearrange("(c p) t -> p c t", p=128)[:, fc, 0:1024], g0t[:, :],
                          cast=True)
                nc.gpsimd.wait_ge(c.st_sem, c.st_cnt)
                c.bar()
            c.finish()
            return nc
        if os.environ.get("A8B_RAWY"):
            for dt in range(8):
                for half in range(2):
                    sl = slice(half * 1024, (half + 1) * 1024)
                    cv.tensor_copy(g0t[:, :], yv[:, dt, sl])
                    c.bar()
                    c.dma_out(g0T_o.rearrange("(c p) t -> p c t", p=128)[:, dt, sl], g0t[:, :],
                              cast=True)
                    nc.gpsimd.wait_ge(c.st_sem, c.st_cnt)
                    c.bar()
            c.finish()
            return nc
        for dt in range(8):
            for q in range(4):
                ii = nc.tensor.matmul(pa[:, :], eb2b[:, dt * 128:(dt + 1) * 128],
                                      wslb[:, q * 512:(q + 1) * 512], start=True, stop=True)
                pemark(ii)
                nc.vector.wait_ge(pe, pec[0])
                cv.tensor_tensor(yv[:, dt, q * 512:(q + 1) * 512], pa[:, :],
                                 yv[:, dt, q * 512:(q + 1) * 512], ALU.add)
                nc.tensor.wait_ge(cv.sem, cv.n)
        c.bar()
        for dt in range(8):
            for half in range(2):
                sl = slice(half * 1024, (half + 1) * 1024)
                cv.tensor_tensor(g0t[:, :], yv[:, dt, sl], xtv[:, dt, sl], ALU.add)
                c.bar()
                cs.activation(g0t[:, :], g0t[:, :], AF.Silu)
                c.bar()
                c.dma_out(g0T_o.rearrange("(c p) t -> p c t", p=128)[:, dt, sl], g0t[:, :],
                          cast=True)
                nc.gpsimd.wait_ge(c.st_sem, c.st_cnt)
                c.bar()
        c.finish()
    return nc



def bld_A9():
    nc = new_nc()
    x = ld(nc, "x", (TOK, D))
    s0 = ld(nc, "s0", (TOK, D))
    g0 = ld(nc, "g0", (TOK, D), BF16)
    x1_o = st(nc, "x1", (TOK, D))
    xn1_o = st(nc, "xn1", (TOK, D))
    with (
        nc.sbuf_tensor("xt", [128, D], F32) as xt,
        nc.sbuf_tensor("s0t", [128, D], F32) as s0t,
        nc.sbuf_tensor("g0t", [128, D], BF16) as g0t,
        nc.sbuf_tensor("x1t", [128, D], F32) as x1t,
        nc.sbuf_tensor("xnt", [128, D], F32) as xnt,
        nc.sbuf_tensor("stats", [128, 8], F32) as stats,
        nc.sbuf_tensor("cst", [128, 2], F32) as cst,
    ):
        c = Ctx(nc)
        c.init()
        cv = c.chained(nc.vector, "cv")
        cs = c.chained(nc.scalar, "cs")
        c.setup_consts(cst, [EPS])
        for i in range(16):
            c.dma_in(xt[:, :], x.rearrange("(i p) d -> p i d", p=128)[:, i])
            c.dma_in(s0t[:, :], s0.rearrange("(i p) d -> p i d", p=128)[:, i])
            c.dma_in(g0t[:, :], g0.rearrange("(i p) d -> p i d", p=128)[:, i])
            nc.sync.wait_ge(c.ld_sem, c.ld_cnt)
            c.bar()
            cv.tensor_mul(x1t[:, :], s0t[:, :], g0t[:, :])
            cv.tensor_add(x1t[:, :], x1t[:, :], xt[:, :])
            c.bar()
            ln_tile(nc, cv, cs, xnt[:, :], x1t[:, :], stats, 0, c.const(EPS))
            c.bar()
            c.dma_out(x1_o.rearrange("(i p) d -> p i d", p=128)[:, i], x1t[:, :])
            c.dma_out(xn1_o.rearrange("(i p) d -> p i d", p=128)[:, i], xnt[:, :])
            nc.sync.wait_ge(c.st_sem, c.st_cnt)
            c.bar()
        c.finish()
    return nc



def bld_A10():
    nc = new_nc()
    xn1T = ld(nc, "xn1T", (D, TOK))
    ipwT = ld(nc, "ipwT", (D, 3072))
    ipb_row = ld(nc, "ipb_row", (1, 3072))
    ctx_o = st(nc, "ctx", (TOK, D))
    with (
        nc.sbuf_tensor("xb", [128, 8 * TOK], BF16) as xb,
        nc.sbuf_tensor("wb", [128, 8 * 3072], BF16) as wb,
        nc.sbuf_tensor("ipb", [1, 3072], BF16) as ipb,
        nc.sbuf_tensor("onesb", [1, 512], BF16) as onesb,
        nc.sbuf_tensor("mx0", [128, 3072], F32) as mx0,
        nc.sbuf_tensor("mx1", [128, 3072], F32) as mx1,
        nc.sbuf_tensor("mx2", [128, 3072], F32) as mx2,
        nc.sbuf_tensor("mx3", [128, 3072], F32) as mx3,
        nc.sbuf_tensor("sctmp", [128, 1024], F32) as sctmp,
        nc.sbuf_tensor("scs", [128, 256], F32) as scs,
        nc.sbuf_tensor("mxs", [128, 64], F32) as mxs,
        nc.sbuf_tensor("ctxs", [128, 4 * D], F32) as ctxs,
        nc.psum_tensor("pa", [128, 512], F32) as pa,
        nc.psum_tensor("pb", [128, 512], F32) as pb,
    ):
        c = Ctx(nc)
        c.init()
        cv = c.chained(nc.vector, "cv")
        cs = c.chained(nc.scalar, "cs")
        pe = c.sem("pe")
        pec = [0]

        def pemark(i):
            i.then_inc(pe, 1)
            pec[0] += 1

        c.dma_in(xb[:, :].rearrange("p (c t) -> p c t", c=8),
                 xn1T.rearrange("(c p) t -> p c t", p=128), cast=True)
        c.dma_in(wb[:, :].rearrange("p (c z) -> p c z", c=8),
                 ipwT.rearrange("(c p) z -> p c z", p=128), cast=True)
        c.dma_in(ipb[:, :], ipb_row, cast=True)
        cv.memset(onesb[:, :], 1.0)
        c.wait_in()
        xv = xb[:, :].rearrange("p (c t) -> p c t", c=8)
        wv = wb[:, :].rearrange("p (c z) -> p c z", c=8)
        mixed = [mx0, mx1, mx2, mx3]
        for j in range(4):
            for b in range(4):
                tcols = b * 512 + j * 128
                for zq in range(6):
                    for dc in range(8):
                        nc.tensor.matmul(pa[:, :], xv[:, dc, tcols:tcols + 128],
                                         wv[:, dc, zq * 512:(zq + 1) * 512],
                                         start=(dc == 0), stop=False)
                    ii = nc.tensor.matmul(pa[:, :], onesb[:1, 0:128],
                                          ipb[:1, zq * 512:(zq + 1) * 512],
                                          start=False, stop=True)
                    pemark(ii)
                    nc.vector.wait_ge(pe, pec[0])
                    cv.tensor_copy(mixed[b][:, zq * 512:(zq + 1) * 512], pa[:, :])
                    nc.tensor.wait_ge(cv.sem, cv.n)
            c.bar()
            for i in range(4):
                qv = mixed[i][:, :].rearrange("p (h z) -> p h z", h=16)
                for jj in range(4):
                    kv = mixed[jj][:, :].rearrange("p (h z) -> p h z", h=16)
                    cv.tensor_tensor(sctmp[:, :].rearrange("p (h z) -> p h z", h=16),
                                     qv[:, :, 0:64], kv[:, :, 64:128], ALU.mult)
                    cv.reduce_sum(
                        bass.AP(scs, i * 64 + jj, [[256, 128], [4, 16]]),
                        sctmp[:, :].rearrange("p (h z) -> p h z", h=16),
                        axis=mybir.AxisListType.X)
            c.bar()
            scv = scs[:, :].rearrange("p (i h j) -> p i h j", i=4, h=16)
            mxv = mxs[:, :].rearrange("p (i h) -> p i h", i=4)
            for i in range(4):
                cv.reduce_max(mxv[:, i, :], scv[:, i, :, :], axis=mybir.AxisListType.X)
            c.bar()
            for i in range(4):
                for h in range(16):
                    cv.tensor_scalar(scv[:, i, h, :], scv[:, i, h, :],
                                     mxv[:, i, h:h + 1], 0.125, ALU.subtract, ALU.mult)
            c.bar()
            cs.activation(scs[:, :], scs[:, :], AF.Exp)
            c.bar()
            for i in range(4):
                cv.reduce_sum(mxv[:, i, :], scv[:, i, :, :], axis=mybir.AxisListType.X)
            cv.reciprocal(mxs[:, :], mxs[:, :])
            c.bar()
            cv.memset(ctxs[:, :], 0.0)
            ctv = ctxs[:, :].rearrange("p (i d) -> p i d", i=4)
            for i in range(4):
                for jj in range(4):
                    vv = mixed[jj][:, :].rearrange("p (h z) -> p h z", h=16)
                    for h in range(16):
                        cv.scalar_tensor_tensor(
                            ctv[:, i, h * 64:(h + 1) * 64],
                            vv[:, h, 128:192],
                            bass.AP(scs, i * 64 + h * 4 + jj, [[256, 128], [1, 1]]),
                            ctv[:, i, h * 64:(h + 1) * 64], ALU.mult, ALU.add)
                for h in range(16):
                    cv.tensor_scalar_mul(ctv[:, i, h * 64:(h + 1) * 64],
                                         ctv[:, i, h * 64:(h + 1) * 64],
                                         mxv[:, i, h:h + 1])
            c.bar()
            for b in range(4):
                c.dma_out(ctx_o.rearrange("(i p) d -> p i d", p=128)[:, b * 4 + j],
                          ctv[:, b, :])
            nc.sync.wait_ge(c.st_sem, c.st_cnt)
            c.bar()
        c.finish()
    return nc



def bld_A11():
    nc = new_nc()
    ctxT = ld(nc, "ctxT", (D, TOK))
    opwT = ld(nc, "opwT", (D, D))
    opb_row = ld(nc, "opb_row", (1, D))
    xn1 = ld(nc, "xn1", (TOK, D))
    ng_row = ld(nc, "ng_row", (1, D))
    nb_row = ld(nc, "nb_row", (1, D))
    semb = ld(nc, "semb", (1024, D))
    sg_row = ld(nc, "sg_row", (1, D))
    sb_row = ld(nc, "sb_row", (1, D))
    scal = ld(nc, "scal", (1, 1))
    s1T_o = st(nc, "s1T", (D, TOK))
    xln_o = st(nc, "xln", (TOK, D))
    se_o = st(nc, "se", (1024, D))
    from contextlib import ExitStack
    with ExitStack() as _es:
        def _sb(nm, shp, dt=F32):
            return _es.enter_context(nc.sbuf_tensor(nm, shp, dt))
        ctb = _sb("ctb", [128, 8 * TOK], BF16)
        opb = _sb("opb", [128, 8 * D], BF16)
        opbr = _sb("opbr", [1, D], BF16)
        onesb = _sb("onesb", [1, 512], BF16)
        ones1 = _sb("ones1", [1, 512], F32)
        xt = _sb("xt", [128, D], F32)
        ot = _sb("ot", [128, D], F32)
        stats = _sb("stats", [128, 8], F32)
        cst = _sb("cst", [128, 2], F32)
        gbc = _sb("gbc", [128, D], F32)
        bbc = _sb("bbc", [128, D], F32)
        sgbc = _sb("sgbc", [128, D], F32)
        sbbc = _sb("sbbc", [128, D], F32)
        scbc = _sb("scbc", [128, 1], F32)
        grow = _sb("grow", [1, 4 * D + 1], F32)
        s1t = _sb("s1t", [128, TOK], F32)
        pa = _es.enter_context(nc.psum_tensor("pa", [128, 512], F32))
        c = Ctx(nc)
        c.init()
        cv = c.chained(nc.vector, "cv")
        cs = c.chained(nc.scalar, "cs")
        pe = c.sem("pe")
        pec = [0]

        def pemark(i):
            i.then_inc(pe, 1)
            pec[0] += 1

        c.dma_in(ctb[:, :].rearrange("p (c t) -> p c t", c=8),
                 ctxT.rearrange("(c p) t -> p c t", p=128), cast=True)
        c.dma_in(opb[:, :].rearrange("p (c d) -> p c d", c=8),
                 opwT.rearrange("(c p) d -> p c d", p=128), cast=True)
        c.dma_in(opbr[:, :], opb_row, cast=True)
        c.dma_in(grow[:, 0:D], ng_row)
        c.dma_in(grow[:, D:2 * D], nb_row)
        c.dma_in(grow[:, 2 * D:3 * D], sg_row)
        c.dma_in(grow[:, 3 * D:4 * D], sb_row)
        c.dma_in(grow[:, 4 * D:4 * D + 1], scal)
        cv.memset(onesb[:, :], 1.0)
        cv.memset(ones1[:, :], 1.0)
        c.setup_consts(cst, [EPS])
        c.wait_in()
        for dst, off in ((gbc, 0), (bbc, D), (sgbc, 2 * D), (sbbc, 3 * D)):
            for half in range(2):
                ii = nc.tensor.matmul(pa[:, :], ones1[:1, 0:128],
                                      grow[:, off + half * 512:off + (half + 1) * 512],
                                      start=True, stop=True)
                pemark(ii)
                nc.vector.wait_ge(pe, pec[0])
                cv.tensor_copy(dst[:, half * 512:(half + 1) * 512], pa[:, :])
                nc.tensor.wait_ge(cv.sem, cv.n)
        ii = nc.tensor.matmul(pa[:, 0:1], ones1[:1, 0:128], grow[:, 4 * D:4 * D + 1],
                              start=True, stop=True)
        pemark(ii)
        nc.vector.wait_ge(pe, pec[0])
        cv.tensor_copy(scbc[:, :], pa[:, 0:1])
        nc.tensor.wait_ge(cv.sem, cv.n)
        c.bar()
        cv2 = ctb[:, :].rearrange("p (c t) -> p c t", c=8)
        ov = opb[:, :].rearrange("p (c d) -> p c d", c=8)
        for dt in range(8):
            for q in range(4):
                for kc in range(8):
                    nc.tensor.matmul(pa[:, :], ov[:, kc, dt * 128:(dt + 1) * 128],
                                     cv2[:, kc, q * 512:(q + 1) * 512],
                                     start=(kc == 0), stop=False)
                ii = nc.tensor.matmul(pa[:, :], opbr[:1, dt * 128:(dt + 1) * 128],
                                      onesb[:1, :], start=False, stop=True)
                pemark(ii)
                nc.vector.wait_ge(pe, pec[0])
                cv.tensor_copy(s1t[:, q * 512:(q + 1) * 512], pa[:, :])
                nc.tensor.wait_ge(cv.sem, cv.n)
            c.bar()
            c.dma_out(s1T_o.rearrange("(c p) t -> p c t", p=128)[:, dt], s1t[:, :])
            nc.sync.wait_ge(c.st_sem, c.st_cnt)
            c.bar()
        for i in range(16):
            c.dma_in(xt[:, :], xn1.rearrange("(i p) d -> p i d", p=128)[:, i])
            nc.sync.wait_ge(c.ld_sem, c.ld_cnt)
            c.bar()
            ln_tile(nc, cv, cs, ot[:, :], xt[:, :], stats, 0, c.const(EPS),
                    gb=(gbc[:, :], bbc[:, :]))
            c.bar()
            c.dma_out(xln_o.rearrange("(i p) d -> p i d", p=128)[:, i], ot[:, :])
            nc.sync.wait_ge(c.st_sem, c.st_cnt)
            c.bar()
        for i in range(8):
            c.dma_in(xt[:, :], semb.rearrange("(i p) d -> p i d", p=128)[:, i])
            nc.sync.wait_ge(c.ld_sem, c.ld_cnt)
            c.bar()
            ln_tile(nc, cv, cs, ot[:, :], xt[:, :], stats, 0, c.const(EPS),
                    gb=(sgbc[:, :], sbbc[:, :]))
            cv.tensor_scalar_mul(ot[:, :], ot[:, :], scbc[:, :])
            c.bar()
            c.dma_out(se_o.rearrange("(i p) d -> p i d", p=128)[:, i], ot[:, :])
            nc.sync.wait_ge(c.st_sem, c.st_cnt)
            c.bar()
        c.finish()
    return nc



def bld_A12a():
    nc = new_nc()
    xlnT = ld(nc, "xlnT", (D, TOK))
    seT = ld(nc, "seT", (D, 1024))
    lmax_o = st(nc, "lmax", (128, 32))
    with (
        nc.sbuf_tensor("xb", [128, 8 * TOK], BF16) as xb,
        nc.sbuf_tensor("sb", [128, 8 * 1024], BF16) as sbb,
        nc.sbuf_tensor("lm", [128, 32], F32) as lm,
        nc.psum_tensor("pa", [128, 512], F32) as pa,
        nc.psum_tensor("pb", [128, 512], F32) as pb,
    ):
        c = Ctx(nc)
        c.init()
        cv = c.chained(nc.vector, "cv")
        pe = c.sem("pe")
        pec = [0]
        c.dma_in(xb[:, :].rearrange("p (c t) -> p c t", c=8),
                 xlnT.rearrange("(c p) t -> p c t", p=128), cast=True)
        c.dma_in(sbb[:, :].rearrange("p (c s) -> p c s", c=8),
                 seT.rearrange("(c p) s -> p c s", p=128), cast=True)
        c.wait_in()
        xv = xb[:, :].rearrange("p (c t) -> p c t", c=8)
        sv = sbb[:, :].rearrange("p (c s) -> p c s", c=8)
        for stt in range(8):
            for b in range(4):
                ps = (pa, pb)[b % 2]
                for ch in range(8):
                    ii = nc.tensor.matmul(ps[:, :], sv[:, ch, stt * 128:(stt + 1) * 128],
                                          xv[:, ch, b * 512:(b + 1) * 512],
                                          start=(ch == 0), stop=(ch == 7))
                ii.then_inc(pe, 1)
                pec[0] += 1
                nc.vector.wait_ge(pe, pec[0])
                cv.reduce_max(lm[:, stt * 4 + b:stt * 4 + b + 1], ps[:, :],
                              axis=mybir.AxisListType.X)
                nc.tensor.wait_ge(cv.sem, cv.n)
        c.bar()
        c.dma_out(lmax_o, lm[:, :])
        c.finish()
    return nc



def bld_A12b():
    nc = new_nc()
    lmax_all = ld(nc, "lmax_all", (128, 8 * 32))
    gmax_o = st(nc, "gmax", (128, 32))
    with (
        nc.sbuf_tensor("lm", [128, 8 * 32], F32) as lm,
        nc.sbuf_tensor("gm", [128, 32], F32) as gm,
    ):
        c = Ctx(nc)
        c.init()
        cv = c.chained(nc.vector, "cv")
        c.dma_in(lm[:, :], lmax_all)
        c.wait_in()
        cv.tensor_copy(gm[:, :], lm[:, 0:32])
        for r in range(1, 8):
            cv.tensor_max(gm[:, :], gm[:, :], lm[:, r * 32:(r + 1) * 32])
        c.bar()
        c.dma_out(gmax_o, gm[:, :])
        c.finish()
    return nc



def bld_A12c():
    nc = new_nc()
    xlnT = ld(nc, "xlnT", (D, TOK))
    seT = ld(nc, "seT", (D, 1024))
    gmr = [ld(nc, "gmr%d" % b, (1, 1024)) for b in range(4)]
    etok_o = st(nc, "etok", (TOK, 1024), BF16)
    lsums_o = st(nc, "lsums", (1, 4 * 1024))
    comb_o = st(nc, "comb", (TOK, 1024), BF16)
    from contextlib import ExitStack
    with ExitStack() as _es:
        def _sb(nm, shp, dt=F32):
            return _es.enter_context(nc.sbuf_tensor(nm, shp, dt))
        xb = _sb("xb", [128, 8 * TOK], BF16)
        sbb = _sb("sbb", [128, 8 * 1024], BF16)
        gmrs = _sb("gmrs", [1, 4 * 1024], F32)
        gbc = _sb("gbc", [128, 4 * 1024], F32)
        ones1 = _sb("ones1", [1, 512], F32)
        onesc = _sb("onesc", [128, 1], BF16)
        lgt = _sb("lgt", [128, 1024], F32)
        et = _sb("et", [128, 1024], BF16)
        ect = _sb("ect", [128, 1024], F32)
        cmt = _sb("cmt", [128, 1024], BF16)
        lsums = _sb("lsumss", [1, 4 * 1024], F32)
        stats = _sb("stats", [128, 8], F32)
        pa = _es.enter_context(nc.psum_tensor("pa", [128, 512], F32))
        pr = _es.enter_context(nc.psum_tensor("pr", [1, 512], F32))
        c = Ctx(nc)
        c.init()
        cv = c.chained(nc.vector, "cv")
        cs = c.chained(nc.scalar, "cs")
        pe = c.sem("pe")
        pec = [0]

        def pemark(i):
            i.then_inc(pe, 1)
            pec[0] += 1

        c.dma_in(xb[:, :].rearrange("p (c t) -> p c t", c=8),
                 xlnT.rearrange("(c p) t -> p c t", p=128), cast=True)
        c.dma_in(sbb[:, :].rearrange("p (c s) -> p c s", c=8),
                 seT.rearrange("(c p) s -> p c s", p=128), cast=True)
        for b in range(4):
            c.dma_in(gmrs[:, b * 1024:(b + 1) * 1024], gmr[b])
        cv.memset(onesc[:, :], 1.0)
        cv.memset(ones1[:, :], 1.0)
        cv.memset(lsums[:, :], 0.0)
        c.wait_in()
        for b in range(4):
            for half in range(2):
                ii = nc.tensor.matmul(pa[:, :], ones1[:1, 0:128],
                                      gmrs[:, b * 1024 + half * 512:b * 1024 + (half + 1) * 512],
                                      start=True, stop=True)
                pemark(ii)
                nc.vector.wait_ge(pe, pec[0])
                cv.tensor_copy(gbc[:, b * 1024 + half * 512:b * 1024 + (half + 1) * 512],
                               pa[:, :])
                nc.tensor.wait_ge(cv.sem, cv.n)
        c.bar()
        xv = xb[:, :].rearrange("p (c t) -> p c t", c=8)
        sv = sbb[:, :].rearrange("p (c s) -> p c s", c=8)
        for i in range(16):
            b = i // 4
            for half in range(2):
                for ch in range(8):
                    ii = nc.tensor.matmul(pa[:, :], xv[:, ch, i * 128:(i + 1) * 128],
                                          sv[:, ch, half * 512:(half + 1) * 512],
                                          start=(ch == 0), stop=(ch == 7))
                pemark(ii)
                nc.vector.wait_ge(pe, pec[0])
                cv.tensor_copy(lgt[:, half * 512:(half + 1) * 512], pa[:, :])
                nc.tensor.wait_ge(cv.sem, cv.n)
            c.bar()
            cv.tensor_tensor(ect[:, :], lgt[:, :],
                             gbc[:, b * 1024:(b + 1) * 1024], ALU.subtract)
            c.bar()
            cs.activation(et[:, :], ect[:, :], AF.Exp)
            cv.reduce_max(stats[:, 0:1], lgt[:, :], axis=mybir.AxisListType.X)
            cv.tensor_scalar_mul(stats[:, 1:2], stats[:, 0:1], -1.0)
            c.bar()
            cs.activation(ect[:, :], lgt[:, :], AF.Exp, bias=stats[:, 1:2])
            c.bar()
            cv.reduce_sum(stats[:, 2:3], ect[:, :], axis=mybir.AxisListType.X)
            cv.reciprocal(stats[:, 3:4], stats[:, 2:3])
            cv.tensor_scalar_mul(cmt[:, :], ect[:, :], stats[:, 3:4])
            c.bar()
            for half in range(2):
                ii = nc.tensor.matmul(pr[:, :], onesc[:, :],
                                      et[:, half * 512:(half + 1) * 512],
                                      start=True, stop=True)
                pemark(ii)
                nc.vector.wait_ge(pe, pec[0])
                cv.tensor_add(lsums[:, b * 1024 + half * 512:b * 1024 + (half + 1) * 512],
                              lsums[:, b * 1024 + half * 512:b * 1024 + (half + 1) * 512],
                              pr[:1, :])
                nc.tensor.wait_ge(cv.sem, cv.n)
            c.bar()
            c.dma_out(etok_o.rearrange("(i p) s -> p i s", p=128)[:, i], et[:, :])
            c.dma_out(comb_o.rearrange("(i p) s -> p i s", p=128)[:, i], cmt[:, :])
            nc.sync.wait_ge(c.st_sem, c.st_cnt)
            c.bar()
        c.dma_out(lsums_o, lsums[:, :])
        c.finish()
    return nc



def bld_A12d():
    nc = new_nc()
    srows = [ld(nc, "sr%d" % r, (1, 4096)) for r in range(8)]
    gs_o = st(nc, "gsums", (1, 4096))
    with (
        nc.sbuf_tensor("acc", [1, 4096], F32) as acc,
        nc.sbuf_tensor("row", [1, 4096], F32) as row,
    ):
        c = Ctx(nc)
        c.init()
        cv = c.chained(nc.vector, "cv")
        c.dma_in(acc[:, :], srows[0])
        c.wait_in()
        for r in range(1, 8):
            c.dma_in(row[:, :], srows[r])
            nc.sync.wait_ge(c.ld_sem, c.ld_cnt)
            c.bar()
            cv.tensor_add(acc[:, :], acc[:, :], row[:, :])
            c.bar()
        c.dma_out(gs_o, acc[:, :])
        c.finish()
    return nc



def bld_A13():
    nc = new_nc()
    etok = ld(nc, "etok", (TOK, 1024), BF16)
    xln = ld(nc, "xln", (TOK, D))
    part_o = st(nc, "part", (32 * 128, D))
    with (
        nc.sbuf_tensor("eb", [128, 16 * 1024], BF16) as eb,
        nc.sbuf_tensor("xb", [128, 16 * D], BF16) as xb,
        nc.sbuf_tensor("ob", [128, D], F32) as ob,
        nc.psum_tensor("pa", [128, 512], F32) as pa,
    ):
        c = Ctx(nc)
        c.init()
        cv = c.chained(nc.vector, "cv")
        pe = c.sem("pe")
        pec = [0]
        c.dma_in(eb[:, :].rearrange("p (i s) -> p i s", i=16),
                 etok.rearrange("(i p) s -> p i s", p=128))
        c.dma_in(xb[:, :].rearrange("p (i d) -> p i d", i=16),
                 xln.rearrange("(i p) d -> p i d", p=128), cast=True)
        c.wait_in()
        ev = eb[:, :].rearrange("p (i s) -> p i s", i=16)
        xv = xb[:, :].rearrange("p (i d) -> p i d", i=16)
        for b in range(4):
            for blk in range(8):
                for q in range(2):
                    for ti in range(4):
                        ii = nc.tensor.matmul(pa[:, :], ev[:, b * 4 + ti, blk * 128:(blk + 1) * 128],
                                              xv[:, b * 4 + ti, q * 512:(q + 1) * 512],
                                              start=(ti == 0), stop=(ti == 3))
                    ii.then_inc(pe, 1)
                    pec[0] += 1
                    nc.vector.wait_ge(pe, pec[0])
                    cv.tensor_copy(ob[:, q * 512:(q + 1) * 512], pa[:, :])
                    nc.tensor.wait_ge(cv.sem, cv.n)
                c.bar()
                c.dma_out(part_o.rearrange("(x p) d -> p x d", p=128)[:, blk * 4 + b], ob[:, :])
                nc.sync.wait_ge(c.st_sem, c.st_cnt)
                c.bar()
        c.finish()
    return nc



def bld_A14():
    nc = new_nc()
    part8 = ld(nc, "part8", (8 * 512, D))
    Sc = ld(nc, "Sc", (128, 4))
    slots_o = st(nc, "slots", (128, 4 * D))
    with (
        nc.sbuf_tensor("acc", [128, 4 * D], F32) as acc,
        nc.sbuf_tensor("buf", [128, 4 * D], F32) as buf,
        nc.sbuf_tensor("sc", [128, 4], F32) as scb,
        nc.sbuf_tensor("rs", [128, 4], F32) as rsb,
    ):
        c = Ctx(nc)
        c.init()
        cv = c.chained(nc.vector, "cv")
        pv = part8.rearrange("(r b p) d -> p r b d", r=8, p=128)
        c.dma_in(acc[:, :].rearrange("p (b d) -> p b d", b=4), pv[:, 0])
        c.dma_in(scb[:, :], Sc)
        c.wait_in()
        for r in range(1, 8):
            c.dma_in(buf[:, :].rearrange("p (b d) -> p b d", b=4), pv[:, r])
            nc.sync.wait_ge(c.ld_sem, c.ld_cnt)
            c.bar()
            cv.tensor_add(acc[:, :], acc[:, :], buf[:, :])
            c.bar()
        cv.reciprocal(rsb[:, :], scb[:, :])
        av = acc[:, :].rearrange("p (b d) -> p b d", b=4)
        for b in range(4):
            cv.tensor_scalar_mul(av[:, b, :], av[:, b, :], rsb[:, b:b + 1])
        c.bar()
        c.dma_out(slots_o, acc[:, :])
        c.finish()
    return nc



def bld_A15():
    nc = new_nc()
    slotsT = ld(nc, "slotsT", (4 * 1024, 128))
    Wg = ld(nc, "Wg", (2 * D, 2048))
    Wu = ld(nc, "Wu", (2 * D, 2048))
    Wd = ld(nc, "Wd", (2 * 2048, D))
    bg_row = ld(nc, "bg_row", (1, 2 * 2048))
    bu_row = ld(nc, "bu_row", (1, 2 * 2048))
    bd_row = ld(nc, "bd_row", (1, 2 * D))
    eye = ld(nc, "eye", (128, 128))
    eout_o = st(nc, "eout", (128, 4 * D), BF16)
    from contextlib import ExitStack
    with ExitStack() as _es:
        def _sb(nm, shp, dt=F32):
            return _es.enter_context(nc.sbuf_tensor(nm, shp, dt))
        slb = _sb("slb", [128, 4 * 8 * 128], BF16)
        eyeb = _sb("eyeb", [128, 128], BF16)
        w0 = _sb("w0", [128, 2048], BF16)
        w1 = _sb("w1", [128, 2048], BF16)
        bgr = _sb("bgr", [1, 2 * 2048], BF16)
        bur = _sb("bur", [1, 2 * 2048], BF16)
        bdr = _sb("bdr", [1, 2 * D], BF16)
        ones64 = _sb("ones64", [1, 64], BF16)
        gate = _sb("gate", [64, 2048], BF16)
        up = _sb("up", [64, 2048], BF16)
        gu = _sb("gu", [64, 2048], BF16)
        guT = _sb("guT", [128, 16 * 64], BF16)
        eout = _sb("eouts", [128, 4 * D], BF16)
        pa = _es.enter_context(nc.psum_tensor("pa", [64, 512], F32))
        pt = _es.enter_context(nc.psum_tensor("pt", [128, 64], BF16))
        c = Ctx(nc)
        c.init()
        cv = c.chained(nc.vector, "cv")
        cs = c.chained(nc.scalar, "cs")
        pe = c.sem("pe")
        pec = [0]

        def pemark(i):
            i.then_inc(pe, 1)
            pec[0] += 1

        c.dma_in(slb[:, :].rearrange("p (b c s) -> p b c s", b=4, c=8),
                 slotsT.rearrange("(b c p) s -> p b c s", b=4, p=128), cast=True)
        c.dma_in(eyeb[:, :], eye, cast=True)
        c.dma_in(bgr[:, :], bg_row, cast=True)
        c.dma_in(bur[:, :], bu_row, cast=True)
        c.dma_in(bdr[:, :], bd_row, cast=True)
        cv.memset(ones64[:, :], 1.0)
        c.wait_in()
        slv = slb[:, :].rearrange("p (b c s) -> p b c s", b=4, c=8)
        wgv = Wg.rearrange("(e c p) f -> p e c f", e=2, p=128)
        wuv = Wu.rearrange("(e c p) f -> p e c f", e=2, p=128)
        wdv = Wd.rearrange("(e f p) d -> p e f d", e=2, p=128)
        guv = guT[:, :].rearrange("p (f s) -> p f s", f=16)
        eov = eout[:, :].rearrange("p (b d) -> p b d", b=4)
        pq = [_es.enter_context(nc.psum_tensor("pq%d" % q, [64, 512], F32)) for q in range(4)]
        for el in range(2):
            for b in range(4):
                for dst, wv, brow in ((gate, wgv, bgr), (up, wuv, bur)):
                    for dc in range(8):
                        wbuf = (w0, w1)[dc % 2]
                        nc.gpsimd.wait_ge(pe, pec[0])
                        c.dma_in(wbuf[:, :], wv[:, el, dc, :], cast=True)
                        nc.sync.wait_ge(c.ld_sem, c.ld_cnt)
                        nc.gpsimd.wait_ge(c.ld_sem, c.ld_cnt)
                        c.bar()
                        for q in range(4):
                            nc.tensor.matmul(pq[q][:, :], slv[:, b, dc, el * 64:(el + 1) * 64],
                                             wbuf[:, q * 512:(q + 1) * 512],
                                             start=(dc == 0), stop=False)
                    for q in range(4):
                        ii = nc.tensor.matmul(pq[q][:, :], ones64[:1, :],
                                              brow[:1, el * 2048 + q * 512:el * 2048 + (q + 1) * 512],
                                              start=False, stop=True)
                        pemark(ii)
                        nc.scalar.wait_ge(pe, pec[0])
                        nc.scalar.wait_ge(cv.sem, cv.n)
                        fn = AF.Gelu if dst is gate else AF.Identity
                        cs.activation(dst[0:64, q * 512:(q + 1) * 512], pq[q][0:64, :], fn)
                    nc.tensor.wait_ge(cs.sem, cs.n)
                    c.bar()
                cv.tensor_mul(gu[0:64, :], gate[0:64, :], up[0:64, :])
                c.bar()
                for fc in range(16):
                    ii = nc.tensor.transpose(pt[:, :], gu[0:64, fc * 128:(fc + 1) * 128],
                                             eyeb[0:64, 0:64])
                    pemark(ii)
                    nc.vector.wait_ge(pe, pec[0])
                    cv.tensor_copy(guv[:, fc, :], pt[:, 0:64])
                    nc.tensor.wait_ge(cv.sem, cv.n)
                c.bar()
                for fc in range(16):
                    wbuf = (w0, w1)[fc % 2]
                    nc.gpsimd.wait_ge(pe, pec[0])
                    c.dma_in(wbuf[:, 0:1024], wdv[:, el, fc, :], cast=True)
                    nc.sync.wait_ge(c.ld_sem, c.ld_cnt)
                    nc.gpsimd.wait_ge(c.ld_sem, c.ld_cnt)
                    c.bar()
                    for q in range(2):
                        nc.tensor.matmul(pq[q][:, :], guv[:, fc, :],
                                         wbuf[:, q * 512:q * 512 + 512],
                                         start=(fc == 0), stop=False)
                for q in range(2):
                    ii = nc.tensor.matmul(pq[q][:, :], ones64[:1, :],
                                          bdr[:1, el * D + q * 512:el * D + (q + 1) * 512],
                                          start=False, stop=True)
                    pemark(ii)
                    nc.vector.wait_ge(pe, pec[0])
                    cv.tensor_copy(eout[el * 64:(el + 1) * 64, b * D + q * 512:b * D + (q + 1) * 512],
                                   pq[q][0:64, :])
                nc.tensor.wait_ge(cv.sem, cv.n)
                c.bar()
        c.dma_out(eout_o, eout[:, :])
        c.finish()
    return nc



def bld_A16():
    nc = new_nc()
    eoutA = ld(nc, "eoutA", (1024, 4 * D), BF16)
    combT = ld(nc, "combT", (1024, TOK), BF16)
    s1T = ld(nc, "s1T", (D, TOK))
    x1T = ld(nc, "x1T", (D, TOK))
    x2T_o = st(nc, "x2T", (D, TOK))
    from contextlib import ExitStack
    with ExitStack() as _es:
        def _sb(nm, shp, dt=F32):
            return _es.enter_context(nc.sbuf_tensor(nm, shp, dt))
        eo = _sb("eo", [128, 8 * 4 * D], BF16)
        cb = _sb("cb", [128, 8 * TOK], BF16)
        s1t = _sb("s1t", [128, 512], F32)
        x1t = _sb("x1t", [128, 512], F32)
        g1 = _sb("g1", [128, 512], F32)
        pa = _es.enter_context(nc.psum_tensor("pa", [128, 512], F32))
        c = Ctx(nc)
        c.init()
        cv = c.chained(nc.vector, "cv")
        cs = c.chained(nc.scalar, "cs")
        pe = c.sem("pe")
        pec = [0]
        c.dma_in(eo[:, :].rearrange("p (r b d) -> p r b d", r=8, b=4),
                 eoutA.rearrange("(r p) (b d) -> p r b d", p=128, b=4))
        c.dma_in(cb[:, :].rearrange("p (r t) -> p r t", r=8),
                 combT.rearrange("(r p) t -> p r t", p=128))
        c.wait_in()
        eov = eo[:, :].rearrange("p (r b d) -> p r b d", r=8, b=4)
        cbv = cb[:, :].rearrange("p (r t) -> p r t", r=8)
        s1tv = s1T.rearrange("(c p) t -> p c t", p=128)
        x1tv = x1T.rearrange("(c p) t -> p c t", p=128)
        x2tv = x2T_o.rearrange("(c p) t -> p c t", p=128)
        for dt in range(8):
            for b in range(4):
                tsl = slice(b * 512, (b + 1) * 512)
                c.dma_in(s1t[:, :], s1tv[:, dt, tsl])
                c.dma_in(x1t[:, :], x1tv[:, dt, tsl])
                for r in range(8):
                    ii = nc.tensor.matmul(pa[:, :], eov[:, r, b, dt * 128:(dt + 1) * 128],
                                          cbv[:, r, b * 512:(b + 1) * 512],
                                          start=(r == 0), stop=(r == 7))
                ii.then_inc(pe, 1)
                pec[0] += 1
                nc.scalar.wait_ge(pe, pec[0])
                nc.scalar.wait_ge(cv.sem, cv.n)
                cs.activation(g1[:, :], pa[:, :], AF.Silu)
                nc.tensor.wait_ge(cs.sem, cs.n)
                nc.vector.wait_ge(cs.sem, cs.n)
                nc.vector.wait_ge(c.ld_sem, c.ld_cnt)
                cv.tensor_mul(g1[:, :], g1[:, :], s1t[:, :])
                cv.tensor_add(x1t[:, :], x1t[:, :], g1[:, :])
                c.bar()
                c.dma_out(x2tv[:, dt, tsl], x1t[:, :])
                nc.sync.wait_ge(c.st_sem, c.st_cnt)
                c.bar()
        c.finish()
    return nc


def run(name, bld, in_maps, trace=False):
    if name not in _prog_cache:
        _prog_cache[name] = bld()
    nc = _prog_cache[name]
    res = run_bass_kernel_spmd(nc, in_maps, core_ids=list(range(NCORE)))
    return res.results


def _t(a):
    return np.ascontiguousarray(np.asarray(a).T)


def kernel(**inp):
    f = {k: f32c(v) for k, v in inp.items() if k not in ("L1_attn_key", "L1_attn_value")}
    x_full = f["L0_attn_x"].reshape(B, S, D)

    def slab(full, c, width=W):
        return np.ascontiguousarray(full.reshape(B, S, -1)[:, c * W:(c + 1) * W].reshape(B * W, -1))

    rng8 = range(NCORE)
    r1 = run("A1", bld_A1, [{
        "x": slab(x_full, c),
        "g_row": f["L0_attn_norm_g"][None, :], "b_row": f["L0_attn_norm_b"][None, :],
    } for c in rng8])
    r2 = run("A2", bld_A2, [{
        "xn2T": _t(r1[c]["xn2"]), "BwT": _t(f["L0_attn_B_w"]),
        "Bb_row": f["L0_attn_B_b"][None, :], "Alog_c": f["L0_attn_A_log"][:, None],
        "iota_t": np.arange(c * W, (c + 1) * W, dtype=np.float32)[None, :],
        "iota_l": np.arange(W, dtype=np.float32)[None, :],
    } for c in rng8])
    E_all = np.concatenate([r2[c]["E"] for c in rng8], axis=1)
    r3 = run("A3", bld_A3, [{"E_all": E_all, "A512": r2[0]["A512"]} for c in rng8])
    S_all = r3[0]["S_all"]
    fb_row = np.concatenate([f["L0_attn_ff1_b"], f["L0_attn_ff2_b"],
                             f["L0_attn_ta_b"], f["L0_attn_tb_b"]])[None, :]
    r4 = run("A4", bld_A4, [{
        "yloc": r2[c]["yloc"], "aj1": r2[c]["aj1"],
        "carry": S_all[:, (c - 1) * B:c * B] if c > 0 else np.zeros((N, B), np.float32),
        "Cw": f["L0_attn_C_w"], "CbT": chunk128(f["L0_attn_C_b"]),
        "ff1T": _t(f["L0_attn_ff1_w"]), "ff2T": _t(f["L0_attn_ff2_w"]),
        "taT": _t(f["L0_attn_ta_w"]), "tbT": _t(f["L0_attn_tb_w"]), "fb": fb_row,
    } for c in rng8])
    ffT = {"ff1T": _t(f["L0_attn_ff1_w"]), "ff2T": _t(f["L0_attn_ff2_w"]),
           "taT": _t(f["L0_attn_ta_w"]), "tbT": _t(f["L0_attn_tb_w"])}
    r5 = run("A5", bld_A5, [dict(P=r4[c]["P"], **ffT) for c in rng8])
    r6 = run("A6", bld_A6, [dict(
        P=np.ascontiguousarray(
            np.asarray(r4[c]["P"]).reshape(128, 12, B, W)[:, :, :, :BFIX].reshape(128, 12 * B * BFIX)),
        carry=(np.asarray(r5[c - 1]["hend"], np.float32) if c > 0 else np.zeros((128, 16), np.float32)),
        **ffT) for c in rng8])
    cfc = []
    for c in rng8:
        sw = np.asarray(r5[c]["cfc"])
        fx = np.asarray(r6[c]["cfc"])
        cfc.append(np.concatenate([fx, sw[:, BFIX * 16:]], axis=1))
    r7 = run("A7", bld_A7, [{
        "cfc": cfc[c], "owT": _t(f["L0_attn_out_w"]), "ob_row": f["L0_attn_out_b"][None, :],
        "xn": r1[c]["xn"], "xnT": _t(r1[c]["xn"]),
        "rw1": f["L0_ffn_rw1"], "rb1_row": f["L0_ffn_rb1"][None, :],
        "rw2": f["L0_ffn_rw2"], "rb2_row": f["L0_ffn_rb2"][None, :],
    } for c in rng8])
    r8a = run("A8a", bld_A8a, [{"rlt": _t(r7[c]["rlog"])} for c in rng8])
    uc = np.zeros((8, 8 * 128), np.float32)
    for e in range(8):
        uc[e, e * 128:(e + 1) * 128] = 1.0
    ew1f = f["L0_ffn_ew1"].reshape(8 * D, 2048)
    ew2f = f["L0_ffn_ew2"].reshape(8 * 2048, D)
    r8b = run("A8b", bld_A8b, [{
        "wsl": _t(r8a[c]["w"]), "xnT": _t(r1[c]["xn"]),
        "ew1": ew1f, "eb1": f["L0_ffn_eb1"], "ew2": ew2f, "eb2": f["L0_ffn_eb2"], "uc": uc,
    } for c in rng8])
    r9 = run("A9", bld_A9, [{
        "x": slab(x_full, c), "s0": r7[c]["s0"],
        "g0": np.ascontiguousarray(np.asarray(r8b[c]["g0T"]).T),
    } for c in rng8])
    r10 = run("A10", bld_A10, [{
        "xn1T": _t(r9[c]["xn1"]), "ipwT": _t(f["L1_attn_in_proj_w"]),
        "ipb_row": f["L1_attn_in_proj_b"][None, :],
    } for c in rng8])
    r11 = run("A11", bld_A11, [{
        "ctxT": _t(r10[c]["ctx"]), "opwT": _t(f["L1_attn_out_proj_w"]),
        "opb_row": f["L1_attn_out_proj_b"][None, :], "xn1": r9[c]["xn1"],
        "ng_row": f["L1_ffn_norm_g"][None, :], "nb_row": f["L1_ffn_norm_b"][None, :],
        "semb": f["L1_ffn_slot_embeds"].reshape(1024, D),
        "sg_row": f["L1_ffn_slot_g"][None, :], "sb_row": f["L1_ffn_slot_b"][None, :],
        "scal": f["L1_ffn_scale"].reshape(1, 1),
    } for c in rng8])
    seT = _t(r11[0]["se"])
    r12a = run("A12a", bld_A12a, [{"xlnT": _t(r11[c]["xln"]), "seT": seT} for c in rng8])
    lmax_all = np.concatenate([r12a[c]["lmax"] for c in rng8], axis=1)
    r12b = run("A12b", bld_A12b, [{"lmax_all": lmax_all} for c in rng8])
    gmax = r12b[0]["gmax"]
    gmr = np.ascontiguousarray(gmax.reshape(128, 8, 4).transpose(2, 1, 0).reshape(4, 1024))
    r12c = run("A12c", bld_A12c, [dict(
        {"xlnT": _t(r11[c]["xln"]), "seT": seT},
        **{"gmr%d" % b: gmr[b][None, :] for b in range(4)}) for c in rng8])
    r12d = run("A12d", bld_A12d, [{("sr%d" % r): r12c[r]["lsums"] for r in rng8} for c in rng8])
    gsums = r12d[0]["gsums"].reshape(4, 1024)
    r13 = run("A13", bld_A13, [{"etok": r12c[c]["etok"], "xln": r11[c]["xln"]} for c in rng8])
    r14in = []
    for c in rng8:
        p8 = np.concatenate([r13[r]["part"][c * 512:(c + 1) * 512] for r in rng8], axis=0)
        Sc = np.ascontiguousarray(gsums[:, c * 128:(c + 1) * 128].T)
        r14in.append({"part8": p8, "Sc": Sc})
    r14 = run("A14", bld_A14, r14in)
    r15in = []
    for c in rng8:
        sl = np.stack([_t(r14[c]["slots"][:, b * D:(b + 1) * D]) for b in range(B)])
        r15in.append({
            "slotsT": np.ascontiguousarray(sl.reshape(4 * D, 128)),
            "Wg": f["L1_ffn_Wg"][2 * c:2 * c + 2].reshape(2 * D, 2048),
            "Wu": f["L1_ffn_Wu"][2 * c:2 * c + 2].reshape(2 * D, 2048),
            "Wd": f["L1_ffn_Wd"][2 * c:2 * c + 2].reshape(2 * 2048, D),
            "bg_row": f["L1_ffn_bg"][2 * c:2 * c + 2].reshape(1, 2 * 2048),
            "bu_row": f["L1_ffn_bu"][2 * c:2 * c + 2].reshape(1, 2 * 2048),
            "bd_row": f["L1_ffn_bd"][2 * c:2 * c + 2].reshape(1, 2 * D),
            "eye": np.eye(128, dtype=np.float32),
        })
    r15 = run("A15", bld_A15, r15in)
    eoutA = np.concatenate([np.asarray(r15[c]["eout"]) for c in rng8], axis=0)
    r16 = run("A16", bld_A16, [{
        "eoutA": eoutA,
        "combT": np.ascontiguousarray(np.asarray(r12c[c]["comb"]).T),
        "s1T": r11[c]["s1T"], "x1T": _t(r9[c]["x1"]),
    } for c in rng8])
    out = np.zeros((B, S, D), np.float32)
    for c in rng8:
        out[:, c * W:(c + 1) * W, :] = r16[c]["x2T"].T.reshape(B, W, D)
    return out


# version 02 (final)
# speedup vs baseline: 1.2280x; 1.2280x over previous
import numpy as np
import ml_dtypes
import concourse.bass as bass
import concourse.mybir as mybir
from concourse.bass_utils import run_bass_kernel_spmd

F32 = mybir.dt.float32
BF16 = mybir.dt.bfloat16
AF = mybir.ActivationFunctionType
ALU = mybir.AluOpType
BFD = ml_dtypes.bfloat16
NCORE = 8
B, S, D, N, H = 4, 4096, 1024, 128, 512
W = S // NCORE
TOK = B * W
ZD = 1536
BFIX = 64
EPS = 1e-5

_prog_cache = {}


def f32c(a):
    return np.ascontiguousarray(a, dtype=np.float32)


def chunk128(v):
    return np.ascontiguousarray(v.reshape(-1, 128).T, dtype=np.float32)



def ld(nc, name, shape, dtype=F32):
    return nc.dram_tensor(name, list(shape), dtype, kind="ExternalInput").ap()


def st(nc, name, shape, dtype=F32):
    return nc.dram_tensor(name, list(shape), dtype, kind="ExternalOutput").ap()


class Ctx:

    def __init__(self, nc):
        self.nc = nc
        self._sems = []
        self.ld_sem = None
        self.ld_cnt = 0
        self.st_sem = None
        self.st_cnt = 0

    def sem(self, name):
        s = self.nc.semaphore(name).__enter__()
        self._sems.append(s)
        return s

    def init(self):
        self.ld_sem = self.sem("ldsem")
        self.st_sem = self.sem("stsem")

    def dma_in(self, dst, src, cast=False):
        eng = self.nc.gpsimd if cast else self.nc.sync
        eng.dma_start(dst, src).then_inc(self.ld_sem, 16)
        self.ld_cnt += 16

    def wait_in(self):
        self.nc.sync.wait_ge(self.ld_sem, self.ld_cnt)
        self.nc.gpsimd.wait_ge(self.ld_sem, self.ld_cnt)
        self.nc.all_engine_barrier()

    def dma_out(self, dst, src, cast=False):
        eng = self.nc.gpsimd if cast else self.nc.sync
        eng.dma_start(dst, src).then_inc(self.st_sem, 16)
        self.st_cnt += 16

    def finish(self):
        self.nc.all_engine_barrier()
        self.nc.sync.wait_ge(self.st_sem, self.st_cnt)

    def bar(self):
        self.nc.all_engine_barrier()

    def chained(self, eng, name):
        return Chained(self, eng, name)

    def setup_consts(self, tile, values):
        self.cmap = {}
        for j, v in enumerate(values):
            self.nc.gpsimd.memset(tile[:, j:j + 1], float(v))
            self.cmap[float(v)] = tile[:, j:j + 1]

    def const(self, v):
        return self.cmap[float(v)]


class Chained:

    def __init__(self, ctx, eng, name):
        self.eng = eng
        self.sem = ctx.sem(name)
        self.n = 0

    def __getattr__(self, attr):
        f = getattr(self.eng, attr)

        def wrap(*a, **k):
            if self.n:
                self.eng.wait_ge(self.sem, self.n)
            inst = f(*a, **k)
            inst.then_inc(self.sem, 1)
            self.n += 1
            return inst
        return wrap


def new_nc():
    return bass.Bass("TRN2", target_bir_lowering=False, num_devices=NCORE)


def ln_tile(nc, v, sc, out_ap, x_ap, stats, col, eps_ap, gb=None, dd=D):
    mu = stats[:, col + 0:col + 1]
    m2 = stats[:, col + 1:col + 2]
    vr = stats[:, col + 2:col + 3]
    rs = stats[:, col + 3:col + 4]
    v.reduce_sum(mu, x_ap, axis=mybir.AxisListType.X)
    v.tensor_scalar_mul(mu, mu, 1.0 / dd)
    v.tensor_mul(out_ap, x_ap, x_ap)
    v.reduce_sum(m2, out_ap, axis=mybir.AxisListType.X)
    v.tensor_scalar_mul(m2, m2, 1.0 / dd)
    v.tensor_mul(vr, mu, mu)
    v.tensor_sub(vr, m2, vr)
    nc.all_engine_barrier()
    sc.activation(vr, vr, AF.Sqrt, bias=eps_ap)
    nc.all_engine_barrier()
    v.reciprocal(rs, vr)
    v.tensor_scalar(out_ap, x_ap, mu, rs, ALU.subtract, ALU.mult)
    if gb is not None:
        v.tensor_mul(out_ap, out_ap, gb[0])
        v.tensor_add(out_ap, out_ap, gb[1])



def bld_A1():
    nc = new_nc()
    x = ld(nc, "x", (TOK, D))
    g_row = ld(nc, "g_row", (1, D))
    b_row = ld(nc, "b_row", (1, D))
    xn_o = st(nc, "xn", (TOK, D))
    xn2_o = st(nc, "xn2", (TOK, D))
    with (
        nc.sbuf_tensor("xt", [128, D], F32) as xt,
        nc.sbuf_tensor("xn_t", [128, D], F32) as xnt,
        nc.sbuf_tensor("xn2_t", [128, D], F32) as xn2t,
        nc.sbuf_tensor("stats", [128, 8], F32) as stats,
        nc.sbuf_tensor("cst", [128, 2], F32) as cst,
        nc.sbuf_tensor("grow", [1, D], F32) as grow,
        nc.sbuf_tensor("brow", [1, D], F32) as brow,
        nc.sbuf_tensor("ones1", [1, 512], F32) as ones1,
        nc.sbuf_tensor("gbc", [128, D], F32) as gbc,
        nc.sbuf_tensor("bbc", [128, D], F32) as bbc,
        nc.psum_tensor("pbc", [128, 512], F32) as pbc,
    ):
        c = Ctx(nc)
        c.init()
        cv = c.chained(nc.vector, "cv")
        cs = c.chained(nc.scalar, "cs")
        c.setup_consts(cst, [EPS])
        c.dma_in(grow[:, :], g_row)
        c.dma_in(brow[:, :], b_row)
        cv.memset(ones1[:, :], 1.0)
        c.wait_in()
        for half in range(2):
            nc.tensor.matmul(pbc[:, :], ones1[:1, 0:128], grow[:, half * 512:(half + 1) * 512],
                             start=True, stop=True)
            c.bar()
            cv.tensor_copy(gbc[:, half * 512:(half + 1) * 512], pbc[:, :])
            c.bar()
            nc.tensor.matmul(pbc[:, :], ones1[:1, 0:128], brow[:, half * 512:(half + 1) * 512],
                             start=True, stop=True)
            c.bar()
            cv.tensor_copy(bbc[:, half * 512:(half + 1) * 512], pbc[:, :])
            c.bar()
        for i in range(16):
            c.dma_in(xt[:, :], x.rearrange("(i p) d -> i p d", p=128)[i])
            nc.sync.wait_ge(c.ld_sem, c.ld_cnt)
            c.bar()
            ln_tile(nc, cv, cs, xnt[:, :], xt[:, :], stats, 0, c.const(EPS))
            c.bar()
            ln_tile(nc, cv, cs, xn2t[:, :], xnt[:, :], stats, 4, c.const(EPS), gb=(gbc[:, :], bbc[:, :]))
            c.bar()
            c.dma_out(xn_o.rearrange("(i p) d -> i p d", p=128)[i], xnt[:, :])
            c.dma_out(xn2_o.rearrange("(i p) d -> i p d", p=128)[i], xn2t[:, :])
            nc.sync.wait_ge(c.st_sem, c.st_cnt)
            c.bar()
        c.finish()
    return nc




def bld_A2(stage=99):
    import os
    nc = new_nc()
    xn2T = ld(nc, "xn2T", (D, TOK))
    BwT = ld(nc, "BwT", (D, N))
    Bb_row = ld(nc, "Bb_row", (1, N))
    Alog_c = ld(nc, "Alog_c", (N, 1))
    iota_t = ld(nc, "iota_t", (1, W))
    iota_l = ld(nc, "iota_l", (1, W))
    yloc_o = st(nc, "yloc", (N, TOK))
    E_o = st(nc, "E", (N, B))
    aj1_o = st(nc, "aj1", (N, W))
    A512_o = st(nc, "A512", (N, 1))
    with (
        nc.sbuf_tensor("x2b", [128, 8 * TOK], BF16) as x2b,
        nc.sbuf_tensor("bwb", [128, 8 * N], BF16) as bwb,
        nc.sbuf_tensor("bbr", [1, N], BF16) as bbr,
        nc.sbuf_tensor("onesb", [1, 1024], BF16) as onesb,
        nc.sbuf_tensor("onesf", [1, 512], F32) as onesf,
        nc.sbuf_tensor("ball", [128, TOK], F32) as ball,
        nc.sbuf_tensor("ybuf", [128, TOK], F32) as ybuf,
        nc.sbuf_tensor("alog", [128, 1], F32) as alog,
        nc.sbuf_tensor("sc1", [128, 16], F32) as sc1,
        nc.sbuf_tensor("itrow", [1, 2 * W], F32) as itrow,
        nc.sbuf_tensor("tblt", [128, W], F32) as tblt,
        nc.sbuf_tensor("apow", [128, W], F32) as apow,
        nc.sbuf_tensor("dg", [128, W], F32) as dg,
        nc.sbuf_tensor("aj1s", [128, W], F32) as aj1,
        nc.psum_tensor("pb", [128, 512], F32) as pb,
        nc.psum_tensor("pb2", [128, 512], F32) as pb2,
    ):
        c = Ctx(nc)
        c.init()
        cv = c.chained(nc.vector, "cv")
        cs = c.chained(nc.scalar, "cs")
        pe = c.sem("pe")
        pec = [0]

        def pemark(inst):
            inst.then_inc(pe, 1)
            pec[0] += 1

        c.dma_in(x2b[:, :].rearrange("p (c t) -> p c t", c=8),
                 xn2T.rearrange("(c p) t -> p c t", p=128), cast=True)
        c.dma_in(bwb[:, :].rearrange("p (c n) -> p c n", c=8),
                 BwT.rearrange("(c p) n -> p c n", p=128), cast=True)
        c.dma_in(bbr[:, :], Bb_row, cast=True)
        c.dma_in(alog[:, :], Alog_c)
        c.dma_in(itrow[:, 0:W], iota_t)
        c.dma_in(itrow[:, W:2 * W], iota_l)
        cv.memset(onesb[:, :], 1.0)
        cv.memset(onesf[:, :], 1.0)
        c.wait_in()
        if stage <= 0:
            c.dma_out(E_o, dg[:, 0:B])
            c.finish()
            return nc
        x2v = x2b[:, :].rearrange("p (c t) -> p c t", c=8)
        bwv = bwb[:, :].rearrange("p (c n) -> p c n", c=8)
        for q in range(int(os.environ.get("A2_NQ", "4"))):
            for ch in range(int(os.environ.get("A2_NCH", "8"))):
                nc.tensor.matmul(pb[:, :], bwv[:, ch, :], x2v[:, ch, q * 512:(q + 1) * 512],
                                 start=(ch == 0), stop=False)
            if os.environ.get("A2_NOBIAS"):
                i = nc.tensor.matmul(pb[:, :], bwv[:, 0, :], x2v[:, 0, q * 512:(q + 1) * 512],
                                     start=False, stop=True)
            else:
                i = nc.tensor.matmul(pb[:, :], bbr[:1, :], onesb[:1, 0:512],
                                     start=False, stop=True)
            pemark(i)
            nc.vector.wait_ge(pe, pec[0])
            cv.tensor_copy(ball[:, q * 512:(q + 1) * 512], pb[:, :])
            nc.tensor.wait_ge(cv.sem, cv.n)
        c.bar()
        if stage <= 1:
            c.dma_out(yloc_o, ball[:, :])
            c.finish()
            return nc
        cs.activation(sc1[:, 0:1], alog[:, :], AF.Sigmoid)
        c.bar()
        cs.activation(sc1[:, 1:2], sc1[:, 0:1], AF.Ln)
        c.bar()
        for k in range(9):
            cs.activation(sc1[:, 2 + k:3 + k], sc1[:, 1:2], AF.Exp, scale=float(1 << k))
        cs.activation(sc1[:, 11:12], sc1[:, 1:2], AF.Exp, scale=float(W))
        c.bar()
        i = nc.tensor.matmul(pb2[:, :], onesf[:1, 0:128], itrow[:, 0:512], start=True, stop=True)
        pemark(i)
        nc.vector.wait_ge(pe, pec[0])
        cv.tensor_copy(tblt[:, :], pb2[:, :])
        nc.tensor.wait_ge(cv.sem, cv.n)
        i = nc.tensor.matmul(pb2[:, :], onesf[:1, 0:128], itrow[:, 512:1024], start=True, stop=True)
        pemark(i)
        nc.vector.wait_ge(pe, pec[0])
        cv.tensor_copy(aj1[:, :], pb2[:, :])
        c.bar()
        cv.tensor_scalar_mul(tblt[:, :], tblt[:, :], sc1[:, 1:2])
        c.bar()
        cs.activation(apow[:, :], tblt[:, :], AF.Exp)
        c.bar()
        cv.tensor_scalar_max(dg[:, :], apow[:, :], 1e-8)
        cv.reciprocal(dg[:, :], dg[:, :])
        cv.tensor_mul(dg[:, :], dg[:, :], apow[:, :])
        cv.tensor_scalar(aj1[:, :], aj1[:, :], 1.0, sc1[:, 1:2], ALU.add, ALU.mult)
        c.bar()
        cs.activation(aj1[:, :], aj1[:, :], AF.Exp)
        c.bar()
        if stage <= 2:
            c.dma_out(yloc_o, ball[:, :])
            c.finish()
            return nc
        for b in range(B):
            cv.tensor_mul(ball[:, b * W:(b + 1) * W], ball[:, b * W:(b + 1) * W], dg[:, :])
        c.bar()
        if stage <= 3:
            c.dma_out(yloc_o, ball[:, :])
            c.finish()
            return nc
        cur, oth = ball, ybuf
        for k in range(9):
            s = 1 << k
            ak = sc1[:, 2 + k:3 + k]
            for b in range(B):
                cv.tensor_copy(oth[:, b * W:b * W + s], cur[:, b * W:b * W + s])
                cv.scalar_tensor_tensor(oth[:, b * W + s:(b + 1) * W],
                                        cur[:, b * W:(b + 1) * W - s], ak,
                                        cur[:, b * W + s:(b + 1) * W],
                                        ALU.mult, ALU.add)
            cur, oth = oth, cur
        c.bar()
        ecol = cur[:, :].rearrange("p (b t) -> p b t", b=B)
        cv.tensor_copy(dg[:, 0:B], ecol[:, :, W - 1])
        c.bar()
        c.dma_out(yloc_o, cur[:, :])
        c.dma_out(E_o, dg[:, 0:B])
        c.dma_out(aj1_o, aj1[:, :])
        c.dma_out(A512_o, sc1[:, 11:12])
        c.finish()
    return nc



def bld_A3():
    nc = new_nc()
    E_all = ld(nc, "E_all", (N, NCORE * B))
    A512 = ld(nc, "A512", (N, 1))
    S_o = st(nc, "S_all", (N, NCORE * B))
    with (
        nc.sbuf_tensor("e", [128, NCORE * B], F32) as e,
        nc.sbuf_tensor("s", [128, NCORE * B], F32) as s,
        nc.sbuf_tensor("a5", [128, 1], F32) as a5,
    ):
        c = Ctx(nc)
        c.init()
        cv = c.chained(nc.vector, "cv")
        c.dma_in(e[:, :], E_all)
        c.dma_in(a5[:, :], A512)
        c.wait_in()
        cv.tensor_copy(s[:, 0:B], e[:, 0:B])
        for r in range(1, NCORE):
            cv.scalar_tensor_tensor(s[:, r * B:(r + 1) * B], s[:, (r - 1) * B:r * B],
                                    a5[:, :], e[:, r * B:(r + 1) * B], ALU.mult, ALU.add)
        c.bar()
        c.dma_out(S_o, s[:, :])
        c.finish()
    return nc



def bld_A4():
    nc = new_nc()
    yloc = ld(nc, "yloc", (N, TOK))
    aj1 = ld(nc, "aj1", (N, W))
    carry = ld(nc, "carry", (N, B))
    Cw = ld(nc, "Cw", (D, N))
    CbT = ld(nc, "CbT", (128, 8))
    ff1T = ld(nc, "ff1T", (D + H, H))
    ff2T = ld(nc, "ff2T", (D + H, H))
    taT = ld(nc, "taT", (D + H, H))
    tbT = ld(nc, "tbT", (D + H, H))
    fb = ld(nc, "fb", (1, 4 * H))
    P_o = st(nc, "P", (128, 12 * TOK), BF16)
    with (
        nc.sbuf_tensor("y", [128, TOK], F32) as y,
        nc.sbuf_tensor("aj", [128, W], F32) as aj,
        nc.sbuf_tensor("car", [128, B], F32) as car,
        nc.sbuf_tensor("stb", [128, TOK], BF16) as stb,
        nc.sbuf_tensor("cwb", [128, 8 * N], BF16) as cwb,
        nc.sbuf_tensor("cbt", [128, 8], BF16) as cbt,
        nc.sbuf_tensor("wxt", [128, 8 * ZD], BF16) as wxt,
        nc.sbuf_tensor("tas", [128, 8 * H], F32) as tas,
        nc.sbuf_tensor("tbs", [128, 8 * H], F32) as tbs,
        nc.sbuf_tensor("wcomb", [128, ZD], BF16) as wcomb,
        nc.sbuf_tensor("fbs", [1, 4 * H], F32) as fbs,
        nc.sbuf_tensor("ball", [1, ZD], F32) as ballr,
        nc.sbuf_tensor("bcomb", [1, ZD], BF16) as bcomb,
        nc.sbuf_tensor("onesb", [1, 1024], BF16) as onesb,
        nc.psum_tensor("pw", [128, 512], F32) as pw,
        nc.psum_tensor("pw2", [128, 512], F32) as pw2,
        nc.psum_tensor("pbias", [1, 512], F32) as pbias,
    ):
        c = Ctx(nc)
        c.init()
        cv = c.chained(nc.vector, "cv")
        pe = c.sem("pe")
        pec = [0]

        def pemark(inst):
            inst.then_inc(pe, 1)
            pec[0] += 1

        c.dma_in(y[:, :], yloc)
        c.dma_in(aj[:, :], aj1)
        c.dma_in(car[:, :], carry)
        c.dma_in(cwb[:, :].rearrange("p (c n) -> p c n", c=8),
                 Cw.rearrange("(c p) n -> p c n", p=128), cast=True)
        c.dma_in(cbt[:, :], CbT, cast=True)
        wxv = wxt[:, :].rearrange("p (c z) -> p c z", c=8)
        c.dma_in(wxv[:, :, 0:512], ff1T.rearrange("(c p) h -> p c h", p=128)[:, 0:8], cast=True)
        c.dma_in(wxv[:, :, 512:1024], ff2T.rearrange("(c p) h -> p c h", p=128)[:, 0:8], cast=True)
        c.dma_in(tas[:, :].rearrange("p (c h) -> p c h", c=8),
                 taT.rearrange("(c p) h -> p c h", p=128)[:, 0:8])
        c.dma_in(tbs[:, :].rearrange("p (c h) -> p c h", c=8),
                 tbT.rearrange("(c p) h -> p c h", p=128)[:, 0:8])
        c.dma_in(fbs[:, :], fb)
        cv.memset(onesb[:, :], 1.0)
        c.wait_in()
        for b in range(B):
            cv.scalar_tensor_tensor(y[:, b * W:(b + 1) * W], aj[:, :], car[:, b:b + 1],
                                    y[:, b * W:(b + 1) * W], ALU.mult, ALU.add)
        cv.tensor_copy(stb[:, :], y[:, :])
        cv.scalar_tensor_tensor(wxv[:, :, 1024:1536],
                                tas[:, :].rearrange("p (c h) -> p c h", c=8), 0.1,
                                tbs[:, :].rearrange("p (c h) -> p c h", c=8), ALU.mult, ALU.add)
        fbv = fbs[:, :].rearrange("o (r h) -> o r h", r=4)
        cv.tensor_copy(ballr[:, 0:512], fbv[:, 0, :])
        cv.tensor_copy(ballr[:, 512:1024], fbv[:, 1, :])
        cv.scalar_tensor_tensor(ballr[:, 1024:1536], fbv[:, 2, :], 0.1, fbv[:, 3, :],
                                ALU.mult, ALU.add)
        c.bar()
        cwv = cwb[:, :].rearrange("p (c n) -> p c n", c=8)
        for seg in range(3):
            z0 = seg * 512
            for chv in range(8):
                i = nc.tensor.matmul(pw2[:, :], cwv[:, chv, :],
                                     wxv[:, chv, z0:z0 + 512], start=(chv == 0), stop=(chv == 7))
            pemark(i)
            nc.vector.wait_ge(pe, pec[0])
            cv.tensor_copy(wcomb[:, z0:z0 + 512], pw2[:, :])
            nc.tensor.wait_ge(cv.sem, cv.n)
        c.bar()
        for seg in range(3):
            z0 = seg * 512
            for chv in range(8):
                i = nc.tensor.matmul(pbias[:, :], cbt[:, chv:chv + 1], wxv[:, chv, z0:z0 + 512],
                                     start=(chv == 0), stop=(chv == 7))
            pemark(i)
            nc.vector.wait_ge(pe, pec[0])
            cv.tensor_add(ballr[:, z0:z0 + 512], ballr[:, z0:z0 + 512], pbias[:1, :])
            nc.tensor.wait_ge(cv.sem, cv.n)
        cv.tensor_copy(bcomb[:, :], ballr[:, :])
        c.bar()
        pov = P_o.rearrange("p (zt t) -> p zt t", zt=12)
        with nc.sbuf_tensor("pout", [128, 2048], BF16) as pout:
            for zt in range(12):
                for q in range(4):
                    nc.tensor.matmul(pw[:, :], wcomb[:, zt * 128:(zt + 1) * 128],
                                     stb[:, q * 512:(q + 1) * 512], start=True, stop=False)
                    i = nc.tensor.matmul(pw[:, :], bcomb[:1, zt * 128:(zt + 1) * 128],
                                         onesb[:1, 0:512], start=False, stop=True)
                    pemark(i)
                    nc.vector.wait_ge(pe, pec[0])
                    cv.tensor_copy(pout[:, q * 512:(q + 1) * 512], pw[:, :])
                    nc.tensor.wait_ge(cv.sem, cv.n)
                c.bar()
                c.dma_out(pov[:, zt, :], pout[:, :])
                nc.sync.wait_ge(c.st_sem, c.st_cnt)
                c.bar()
        c.finish()
    return nc



def emit_scan(nc, c, cv_unused, P_sb, whv, cfc, pz, zs, gg, tmp, sP, sV, sA, T):
    import os
    parts = os.environ.get("SCAN_PARTS", "all")
    pv = P_sb.rearrange("p (zt b t) -> p zt b t", zt=12, b=B)
    te, ve, se = nc.tensor, nc.vector, nc.scalar
    for t in range(T):
        pzt = pz[t % 2]
        zst = zs[t % 2]
        ggt = gg[t % 2]
        pzv = pzt[:, :].rearrange("p (zt b) -> p zt b", zt=12)
        if t > 0:
            te.wait_ge(sV, 4 * t)
        nacc = int(os.environ.get("SCAN_NACC", "4"))
        for zt in range(12):
            for hc in range(nacc):
                i = te.matmul(pzt[:, zt * 4:(zt + 1) * 4],
                              whv[:, hc, zt * 128:(zt + 1) * 128],
                              cfc[:, t, hc * 4:(hc + 1) * 4],
                              start=(hc == 0), stop=(hc == nacc - 1))
        i.then_inc(sP, 1)
        if parts == "pe":
            ve.wait_ge(sP, t + 1)
            ve.tensor_tensor(cfc[:, t + 1, :], cfc[:, t, :], cfc[:, t, :], ALU.add).then_inc(sV, 4)
            se.nop().then_inc(sA, 2)
            continue
        ve.wait_ge(sP, t + 1)
        if t >= 2:
            ve.wait_ge(sA, 2 * (t - 1))
        ve.tensor_tensor(zst[:, :].rearrange("p (zt b) -> p zt b", zt=12),
                         pzv, pv[:, :, :, t], ALU.add).then_inc(sV, 1)
        if parts == "pedve":
            ve.tensor_copy(cfc[:, t + 1, :], zst[:, 0:16]).then_inc(sV, 3)
            se.nop().then_inc(sA, 2)
            continue
        se.wait_ge(sV, 4 * t + 1)
        af2 = AF.Tanh if os.environ.get("SCAN_TANHONLY") else AF.Sigmoid
        se.activation(ggt[:, 0:32], zst[:, 0:32], AF.Tanh).then_inc(sA, 1)
        se.activation(ggt[:, 32:48], zst[:, 32:48], af2).then_inc(sA, 1)
        ve.wait_ge(sA, 2 * (t + 1))
        ve.tensor_sub(tmp[:, 0:16], ggt[:, 16:32], ggt[:, 0:16]).then_inc(sV, 1)
        ve.wait_ge(sV, 4 * t + 2)
        ve.tensor_mul(tmp[:, 16:32], ggt[:, 32:48], tmp[:, 0:16]).then_inc(sV, 1)
        ve.wait_ge(sV, 4 * t + 3)
        ve.tensor_add(cfc[:, t + 1, :], ggt[:, 0:16], tmp[:, 16:32]).then_inc(sV, 1)


def bld_scan(T, with_carry, stage=99):
    import os
    nc = new_nc()
    P_in = ld(nc, "P", (128, 12 * B * T), BF16)
    ff1T = ld(nc, "ff1T", (D + H, H))
    ff2T = ld(nc, "ff2T", (D + H, H))
    taT = ld(nc, "taT", (D + H, H))
    tbT = ld(nc, "tbT", (D + H, H))
    if with_carry:
        carry = ld(nc, "carry", (128, 16))
    cfc_o = st(nc, "cfc", (128, T * 16), BF16)
    hend_o = st(nc, "hend", (128, 16))
    with (
        nc.sbuf_tensor("psb", [128, 12 * B * T], BF16) as psb,
        nc.sbuf_tensor("wht", [128, 4 * ZD], BF16) as wht,
        nc.sbuf_tensor("tas", [128, 4 * H], F32) as tas,
        nc.sbuf_tensor("tbs", [128, 4 * H], F32) as tbs,
        nc.sbuf_tensor("cfcs", [128, (T + 1) * 16], BF16) as cfcs,
        nc.sbuf_tensor("zsA", [128, 48], F32) as zsA,
        nc.sbuf_tensor("zsB", [128, 48], F32) as zsB,
        nc.sbuf_tensor("ggA", [128, 48], F32) as ggA,
        nc.sbuf_tensor("ggB", [128, 48], F32) as ggB,
        nc.sbuf_tensor("tmp", [128, 32], F32) as tmp,
        nc.sbuf_tensor("hef", [128, 16], F32) as hef,
        nc.psum_tensor("pzA", [128, 48], F32) as pzA,
        nc.psum_tensor("pzB", [128, 48], F32) as pzB,
    ):
        c = Ctx(nc)
        c.init()
        cv = c.chained(nc.vector, "cv")
        sP = c.sem("sP")
        sV = c.sem("sV")
        sA = c.sem("sA")
        c.dma_in(psb[:, :], P_in)
        whv = wht[:, :].rearrange("p (hc z) -> p hc z", hc=4)
        c.dma_in(whv[:, :, 0:512], ff1T.rearrange("(c p) h -> p c h", p=128)[:, 8:12], cast=True)
        c.dma_in(whv[:, :, 512:1024], ff2T.rearrange("(c p) h -> p c h", p=128)[:, 8:12], cast=True)
        c.dma_in(tas[:, :].rearrange("p (c h) -> p c h", c=4),
                 taT.rearrange("(c p) h -> p c h", p=128)[:, 8:12])
        c.dma_in(tbs[:, :].rearrange("p (c h) -> p c h", c=4),
                 tbT.rearrange("(c p) h -> p c h", p=128)[:, 8:12])
        if with_carry:
            c.dma_in(hef[:, :], carry)
        nc.scalar.preload_act_table(AF.Sigmoid)
        c.wait_in()
        if stage <= 0:
            cv.tensor_copy(hef[:, :], tas[:, 0:16])
            c.bar()
            c.dma_out(cfc_o, cfcs[:, 16:])
            c.dma_out(hend_o, hef[:, :])
            c.finish()
            return nc
        cv.scalar_tensor_tensor(whv[:, :, 1024:1536],
                                tas[:, :].rearrange("p (c h) -> p c h", c=4), 0.1,
                                tbs[:, :].rearrange("p (c h) -> p c h", c=4), ALU.mult, ALU.add)
        cfcv = cfcs[:, :].rearrange("p (t h) -> p t h", h=16)
        if with_carry:
            cv.tensor_copy(cfcv[:, 0, :], hef[:, :])
        else:
            cv.memset(cfcv[:, 0, :], 0.0)
        c.bar()
        if stage <= 1:
            cv.tensor_copy(hef[:, :], tas[:, 0:16])
            c.bar()
            c.dma_out(cfc_o, cfcs[:, 16:])
            c.dma_out(hend_o, hef[:, :])
            c.finish()
            return nc
        emit_scan(nc, c, cv, psb[:, :], whv, cfcv, (pzA, pzB), (zsA, zsB), (ggA, ggB),
                  tmp, sP, sV, sA, T)
        c.bar()
        cv.tensor_copy(hef[:, :], cfcv[:, T, :])
        c.bar()
        c.dma_out(cfc_o, cfcs[:, 16:])
        c.dma_out(hend_o, hef[:, :])
        c.finish()
    return nc


def bld_A5():
    return bld_scan(W, False)


def bld_A6():
    return bld_scan(BFIX, True)



def bld_A7(stage=99):
    import os
    nc = new_nc()
    cfc = ld(nc, "cfc", (128, W * 16), BF16)
    owT = ld(nc, "owT", (H, D))
    ob_row = ld(nc, "ob_row", (1, D))
    xn = ld(nc, "xn", (TOK, D))
    xnT = ld(nc, "xnT", (D, TOK))
    rw1 = ld(nc, "rw1", (D, D))
    rb1_row = ld(nc, "rb1_row", (1, D))
    rw2 = ld(nc, "rw2", (D, 8))
    rb2_row = ld(nc, "rb2_row", (1, 8))
    s0_o = st(nc, "s0", (TOK, D))
    rlog_o = st(nc, "rlog", (8, TOK))
    with (
        nc.sbuf_tensor("cfcs", [128, W * 16], BF16) as cfcs,
        nc.sbuf_tensor("owb", [128, 4 * D], BF16) as owb,
        nc.sbuf_tensor("obr", [1, D], BF16) as obr,
        nc.sbuf_tensor("xntb", [128, 16 * D], BF16) as xntb,
        nc.sbuf_tensor("xnTb", [128, 8 * TOK], BF16) as xnTb,
        nc.sbuf_tensor("rw1b", [128, 8 * D], BF16) as rw1b,
        nc.sbuf_tensor("rb1b", [1, D], BF16) as rb1b,
        nc.sbuf_tensor("rw2b", [128, 8 * 8], BF16) as rw2b,
        nc.sbuf_tensor("rb2b", [1, 8], BF16) as rb2b,
        nc.sbuf_tensor("onesb", [1, 512], BF16) as onesb,
        nc.sbuf_tensor("s0t", [128, D], F32) as s0t,
        nc.sbuf_tensor("lts", [128, 8 * TOK], BF16) as lts,
        nc.sbuf_tensor("rls", [8, TOK], F32) as rls,
        nc.psum_tensor("ps", [128, 512], F32) as ps,
        nc.psum_tensor("pl", [8, 512], F32) as pl,
    ):
        c = Ctx(nc)
        c.init()
        cv = c.chained(nc.vector, "cv")
        cs = c.chained(nc.scalar, "cs")
        pe = c.sem("pe")
        pec = [0]

        def pemark(i):
            i.then_inc(pe, 1)
            pec[0] += 1

        c.dma_in(cfcs[:, :], cfc)
        c.dma_in(owb[:, :].rearrange("p (c d) -> p c d", c=4),
                 owT.rearrange("(c p) d -> p c d", p=128), cast=True)
        c.dma_in(obr[:, :], ob_row, cast=True)
        c.dma_in(xntb[:, :].rearrange("p (i d) -> p i d", i=16),
                 xn.rearrange("(i p) d -> p i d", p=128), cast=True)
        c.dma_in(xnTb[:, :].rearrange("p (c t) -> p c t", c=8),
                 xnT.rearrange("(c p) t -> p c t", p=128), cast=True)
        c.dma_in(rw1b[:, :].rearrange("p (c f) -> p c f", c=8),
                 rw1.rearrange("(c p) f -> p c f", p=128), cast=True)
        c.dma_in(rb1b[:, :], rb1_row, cast=True)
        c.dma_in(rw2b[:, :].rearrange("p (c e) -> p c e", c=8),
                 rw2.rearrange("(c p) e -> p c e", p=128), cast=True)
        c.dma_in(rb2b[:, :], rb2_row, cast=True)
        cv.memset(onesb[:, :], 1.0)
        c.wait_in()
        cfcv = cfcs[:, :].rearrange("p (t h) -> p t h", h=16)
        owv = owb[:, :].rearrange("p (c d) -> p c d", c=4)
        xnv = xntb[:, :].rearrange("p (i d) -> p i d", i=16)
        for i in range(16):
            b = i // 4
            t0 = (i % 4) * 128
            for q in range(2):
                for hc in range(4):
                    nc.tensor.matmul(ps[:, :], cfcv[:, t0:t0 + 128, hc * 4 + b],
                                     owv[:, hc, q * 512:(q + 1) * 512],
                                     start=(hc == 0), stop=False)
                ii = nc.tensor.matmul(ps[:, :], onesb[:1, 0:128], obr[:1, q * 512:(q + 1) * 512],
                                      start=False, stop=True)
                pemark(ii)
                nc.vector.wait_ge(pe, pec[0])
                cv.tensor_tensor(s0t[:, q * 512:(q + 1) * 512], ps[:, :],
                                 xnv[:, i, q * 512:(q + 1) * 512], ALU.add)
                nc.tensor.wait_ge(cv.sem, cv.n)
            c.bar()
            c.dma_out(s0_o.rearrange("(i p) d -> p i d", p=128)[:, i], s0t[:, :])
            nc.sync.wait_ge(c.st_sem, c.st_cnt)
            c.bar()
        c.bar()
        if stage <= 1:
            c.dma_out(rlog_o, rls[:, :])
            c.finish()
            return nc
        xtv = xnTb[:, :].rearrange("p (c t) -> p c t", c=8)
        rv1 = rw1b[:, :].rearrange("p (c f) -> p c f", c=8)
        ltv = lts[:, :].rearrange("p (f t) -> p f t", f=8)
        for ft in range(8):
            for q in range(4):
                for ch in range(8):
                    nc.tensor.matmul(ps[:, :], rv1[:, ch, ft * 128:(ft + 1) * 128],
                                     xtv[:, ch, q * 512:(q + 1) * 512],
                                     start=(ch == 0), stop=False)
                ii = nc.tensor.matmul(ps[:, :], rb1b[:1, ft * 128:(ft + 1) * 128],
                                      onesb[:1, :], start=False, stop=True)
                pemark(ii)
                nc.scalar.wait_ge(pe, pec[0])
                cs.activation(ltv[:, ft, q * 512:(q + 1) * 512], ps[:, :], AF.Silu)
                nc.tensor.wait_ge(cs.sem, cs.n)
        c.bar()
        if stage <= 2:
            c.dma_out(rlog_o, rls[:, :])
            c.finish()
            return nc
        rv2 = rw2b[:, :].rearrange("p (c e) -> p c e", c=8)
        for q in range(4):
            for ch in range(8):
                nc.tensor.matmul(pl[:, :], rv2[:, ch, :], ltv[:, ch, q * 512:(q + 1) * 512],
                                 start=(ch == 0), stop=False)
            ii = nc.tensor.matmul(pl[:, :], rb2b[:1, :], onesb[:1, :],
                                  start=False, stop=True)
            pemark(ii)
            nc.vector.wait_ge(pe, pec[0])
            cv.tensor_copy(rls[:, q * 512:(q + 1) * 512], pl[:, :])
            nc.tensor.wait_ge(cv.sem, cv.n)
        c.bar()
        c.dma_out(rlog_o, rls[:, :])
        c.finish()
    return nc



def bld_A8a():
    nc = new_nc()
    rlt = ld(nc, "rlt", (TOK, 8))
    w_o = st(nc, "w", (TOK, 8))
    with (
        nc.sbuf_tensor("lg", [128, 16 * 8], F32) as lg,
        nc.sbuf_tensor("wv", [128, 16 * 8], F32) as wv,
        nc.sbuf_tensor("sc", [128, 8], F32) as scr,
        nc.sbuf_tensor("negb", [128, 8], F32) as negb,
        nc.sbuf_tensor("eqv", [128, 8], F32) as eqv,
        nc.sbuf_tensor("ev", [128, 8], F32) as ev,
    ):
        c = Ctx(nc)
        c.init()
        cv = c.chained(nc.vector, "cv")
        cs = c.chained(nc.scalar, "cs")
        c.dma_in(lg[:, :].rearrange("p (i e) -> p i e", i=16),
                 rlt.rearrange("(i p) e -> p i e", p=128))
        nc.gpsimd.memset(negb[:, :], -1e30)
        c.wait_in()
        lgv = lg[:, :].rearrange("p (i e) -> p i e", i=16)
        wvv = wv[:, :].rearrange("p (i e) -> p i e", i=16)
        for i in range(16):
            t = lgv[:, i, :]
            cv.reduce_max(scr[:, 0:1], t, axis=mybir.AxisListType.X)
            cv.tensor_scalar(eqv[:, :], t, scr[:, 0:1], None, ALU.is_equal)
            cv.scalar_tensor_tensor(ev[:, :], eqv[:, :], -1e30, t, ALU.mult, ALU.add)
            cv.reduce_max(scr[:, 1:2], ev[:, :], axis=mybir.AxisListType.X)
            cv.tensor_scalar_mul(scr[:, 2:3], scr[:, 0:1], -1.0)
            c.bar()
            cs.activation(ev[:, :], t, AF.Exp, bias=scr[:, 2:3])
            c.bar()
            cv.tensor_scalar(eqv[:, :], t, scr[:, 1:2], None, ALU.is_ge)
            cv.tensor_mul(ev[:, :], ev[:, :], eqv[:, :])
            cv.reduce_sum(scr[:, 3:4], ev[:, :], axis=mybir.AxisListType.X)
            cv.reciprocal(scr[:, 4:5], scr[:, 3:4])
            cv.tensor_scalar_mul(wvv[:, i, :], ev[:, :], scr[:, 4:5])
        c.bar()
        c.dma_out(w_o.rearrange("(i p) e -> p i e", p=128), wvv)
        c.finish()
    return nc



def bld_A8b():
    nc = new_nc()
    wsl = ld(nc, "wsl", (8, TOK))
    xnT = ld(nc, "xnT", (D, TOK))
    ew1 = ld(nc, "ew1", (8 * D, 2048))
    eb1 = ld(nc, "eb1", (8, 2048))
    ew2 = ld(nc, "ew2", (8 * 2048, D))
    eb2 = ld(nc, "eb2", (8, D))
    uc = ld(nc, "uc", (8, 8 * 128))
    g0T_o = st(nc, "g0T", (D, TOK), BF16)
    from contextlib import ExitStack
    with ExitStack() as _es:
        def _sb(nm, shp, dt=F32):
            return _es.enter_context(nc.sbuf_tensor(nm, shp, dt))
        wslb = _sb("wslb", [8, TOK], BF16)
        xnTb = _sb("xnTb", [128, 8 * TOK], BF16)
        eb2b = _sb("eb2b", [8, D], BF16)
        ucb = _sb("ucb", [8, 8 * 128], BF16)
        onesb = _sb("onesb", [1, 512], BF16)
        eb1r = _sb("eb1r", [1, 2048], BF16)
        yT = _sb("yT", [128, 8 * TOK], F32)
        ehw = _sb("ehw", [128, 16 * 1024], BF16)
        wbc = _sb("wbc", [128, 1024], BF16)
        w1b0 = _sb("w1b0", [128, 8 * 128], BF16)
        w1b1 = _sb("w1b1", [128, 8 * 128], BF16)
        w2b0 = _sb("w2b0", [128, 16 * 128], BF16)
        w2b1 = _sb("w2b1", [128, 16 * 128], BF16)
        ehc = _sb("ehc", [128, 1024], BF16)
        g0t = _sb("g0t", [128, 1024], F32)
        pa = _es.enter_context(nc.psum_tensor("pa", [128, 512], F32))
        pb = _es.enter_context(nc.psum_tensor("pb", [128, 512], F32))
        c = Ctx(nc)
        c.init()
        cv = c.chained(nc.vector, "cv")
        cs = c.chained(nc.scalar, "cs")
        pe = c.sem("pe")
        pec = [0]
        wsem = c.sem("wsem")
        wcnt = [0]

        def pemark(i):
            i.then_inc(pe, 1)
            pec[0] += 1

        c.dma_in(wslb[:, :], wsl, cast=True)
        c.dma_in(xnTb[:, :].rearrange("p (c t) -> p c t", c=8),
                 xnT.rearrange("(c p) t -> p c t", p=128), cast=True)
        c.dma_in(eb2b[:, :], eb2, cast=True)
        c.dma_in(ucb[:, :], uc, cast=True)
        cv.memset(onesb[:, :], 1.0)
        cv.memset(yT[:, :], 0.0)
        c.wait_in()
        xtv = xnTb[:, :].rearrange("p (c t) -> p c t", c=8)
        ehv = ehw[:, :].rearrange("p (f t) -> p f t", f=16)
        w1v = [w1b0[:, :].rearrange("p (c f) -> p c f", c=8),
               w1b1[:, :].rearrange("p (c f) -> p c f", c=8)]
        w2v = [w2b0[:, :].rearrange("p (f d) -> p f d", f=16),
               w2b1[:, :].rearrange("p (f d) -> p f d", f=16)]
        yv = yT[:, :].rearrange("p (d t) -> p d t", d=8)
        ew1v = ew1.rearrange("(e c p) f -> p e c f", e=8, p=128)
        ew2v = ew2.rearrange("(e f p) d -> p e f d", e=8, p=128)
        import os
        NSLAB = int(os.environ.get("A8B_NSLAB", "2"))
        NEXP = int(os.environ.get("A8B_NEXP", "8"))
        for slab in range(NSLAB):
            toff = slab * 1024
            for e in range(NEXP):
                for q in range(2):
                    ii = nc.tensor.matmul(pa[:, :], ucb[:, e * 128:(e + 1) * 128],
                                          wslb[:, toff + q * 512:toff + (q + 1) * 512],
                                          start=True, stop=True)
                    pemark(ii)
                    nc.vector.wait_ge(pe, pec[0])
                    cv.tensor_copy(wbc[:, q * 512:(q + 1) * 512], pa[:, :])
                    nc.tensor.wait_ge(cv.sem, cv.n)
                nc.gpsimd.wait_ge(pe, pec[0])
                c.dma_in(eb1r[:, :], eb1[e:e + 1, :], cast=True)
                nc.sync.wait_ge(c.ld_sem, c.ld_cnt)
                c.bar()
                lastuse1 = [0, 0]
                ldready1 = {}
                nc.gpsimd.wait_ge(pe, lastuse1[0])
                c.dma_in(w1v[0], ew1v[:, e, :, 0:128], cast=True)
                ldready1[0] = c.ld_cnt
                for fc in range(16):
                    wb = w1v[fc % 2]
                    if fc + 1 < 16:
                        nc.gpsimd.wait_ge(pe, lastuse1[(fc + 1) % 2])
                        c.dma_in(w1v[(fc + 1) % 2],
                                 ew1v[:, e, :, (fc + 1) * 128:(fc + 2) * 128], cast=True)
                        ldready1[fc + 1] = c.ld_cnt
                    nc.tensor.wait_ge(c.ld_sem, ldready1[fc])
                    for q in range(2):
                        for dc in range(8):
                            nc.tensor.matmul(pb[:, :], wb[:, dc, :],
                                             xtv[:, dc, toff + q * 512:toff + (q + 1) * 512],
                                             start=(dc == 0), stop=False)
                        ii = nc.tensor.matmul(pb[:, :], eb1r[:1, fc * 128:(fc + 1) * 128],
                                              onesb[:1, :], start=False, stop=True)
                        pemark(ii)
                        nc.scalar.wait_ge(pe, pec[0])
                        nc.scalar.wait_ge(cv.sem, cv.n)
                        cs.activation(ehc[:, q * 512:(q + 1) * 512], pb[:, :], AF.Silu)
                        nc.tensor.wait_ge(cs.sem, cs.n)
                    lastuse1[fc % 2] = pec[0]
                    nc.vector.wait_ge(cs.sem, cs.n)
                    cv.tensor_mul(ehv[:, fc, :], ehc[:, :], wbc[:, :])
                c.bar()
                lastuse2 = [0, 0]
                ldready2 = {}
                nc.gpsimd.wait_ge(pe, lastuse2[0])
                c.dma_in(w2v[0], ew2v[:, e, :, 0:128], cast=True)
                ldready2[0] = c.ld_cnt
                for dt in range(8):
                    wb2 = w2v[dt % 2]
                    if dt + 1 < 8:
                        nc.gpsimd.wait_ge(pe, lastuse2[(dt + 1) % 2])
                        c.dma_in(w2v[(dt + 1) % 2],
                                 ew2v[:, e, :, (dt + 1) * 128:(dt + 2) * 128], cast=True)
                        ldready2[dt + 1] = c.ld_cnt
                    nc.tensor.wait_ge(c.ld_sem, ldready2[dt])
                    for q in range(2):
                        for fc in range(16):
                            ii = nc.tensor.matmul(pa[:, :], wb2[:, fc, :],
                                                  ehv[:, fc, q * 512:(q + 1) * 512],
                                                  start=(fc == 0), stop=(fc == 15))
                        pemark(ii)
                        nc.vector.wait_ge(pe, pec[0])
                        cv.tensor_tensor(yv[:, dt, toff + q * 512:toff + (q + 1) * 512],
                                         pa[:, :],
                                         yv[:, dt, toff + q * 512:toff + (q + 1) * 512],
                                         ALU.add)
                        nc.tensor.wait_ge(cv.sem, cv.n)
                    lastuse2[dt % 2] = pec[0]
                c.bar()
        if os.environ.get("A8B_DUMPEH"):
            for fc in range(8):
                cv.tensor_copy(g0t[:, :], ehv[:, fc, :])
                c.bar()
                c.dma_out(g0T_o.rearrange("(c p) t -> p c t", p=128)[:, fc, 0:1024], g0t[:, :],
                          cast=True)
                nc.gpsimd.wait_ge(c.st_sem, c.st_cnt)
                c.bar()
            c.finish()
            return nc
        if os.environ.get("A8B_RAWY"):
            for dt in range(8):
                for half in range(2):
                    sl = slice(half * 1024, (half + 1) * 1024)
                    cv.tensor_copy(g0t[:, :], yv[:, dt, sl])
                    c.bar()
                    c.dma_out(g0T_o.rearrange("(c p) t -> p c t", p=128)[:, dt, sl], g0t[:, :],
                              cast=True)
                    nc.gpsimd.wait_ge(c.st_sem, c.st_cnt)
                    c.bar()
            c.finish()
            return nc
        for dt in range(8):
            for q in range(4):
                ii = nc.tensor.matmul(pa[:, :], eb2b[:, dt * 128:(dt + 1) * 128],
                                      wslb[:, q * 512:(q + 1) * 512], start=True, stop=True)
                pemark(ii)
                nc.vector.wait_ge(pe, pec[0])
                cv.tensor_tensor(yv[:, dt, q * 512:(q + 1) * 512], pa[:, :],
                                 yv[:, dt, q * 512:(q + 1) * 512], ALU.add)
                nc.tensor.wait_ge(cv.sem, cv.n)
        c.bar()
        for dt in range(8):
            for half in range(2):
                sl = slice(half * 1024, (half + 1) * 1024)
                cv.tensor_tensor(g0t[:, :], yv[:, dt, sl], xtv[:, dt, sl], ALU.add)
                c.bar()
                cs.activation(g0t[:, :], g0t[:, :], AF.Silu)
                c.bar()
                c.dma_out(g0T_o.rearrange("(c p) t -> p c t", p=128)[:, dt, sl], g0t[:, :],
                          cast=True)
                nc.gpsimd.wait_ge(c.st_sem, c.st_cnt)
                c.bar()
        c.finish()
    return nc



def bld_A9():
    nc = new_nc()
    x = ld(nc, "x", (TOK, D))
    s0 = ld(nc, "s0", (TOK, D))
    g0 = ld(nc, "g0", (TOK, D), BF16)
    x1_o = st(nc, "x1", (TOK, D))
    xn1_o = st(nc, "xn1", (TOK, D))
    with (
        nc.sbuf_tensor("xt", [128, D], F32) as xt,
        nc.sbuf_tensor("s0t", [128, D], F32) as s0t,
        nc.sbuf_tensor("g0t", [128, D], BF16) as g0t,
        nc.sbuf_tensor("x1t", [128, D], F32) as x1t,
        nc.sbuf_tensor("xnt", [128, D], F32) as xnt,
        nc.sbuf_tensor("stats", [128, 8], F32) as stats,
        nc.sbuf_tensor("cst", [128, 2], F32) as cst,
    ):
        c = Ctx(nc)
        c.init()
        cv = c.chained(nc.vector, "cv")
        cs = c.chained(nc.scalar, "cs")
        c.setup_consts(cst, [EPS])
        for i in range(16):
            c.dma_in(xt[:, :], x.rearrange("(i p) d -> p i d", p=128)[:, i])
            c.dma_in(s0t[:, :], s0.rearrange("(i p) d -> p i d", p=128)[:, i])
            c.dma_in(g0t[:, :], g0.rearrange("(i p) d -> p i d", p=128)[:, i])
            nc.sync.wait_ge(c.ld_sem, c.ld_cnt)
            c.bar()
            cv.tensor_mul(x1t[:, :], s0t[:, :], g0t[:, :])
            cv.tensor_add(x1t[:, :], x1t[:, :], xt[:, :])
            c.bar()
            ln_tile(nc, cv, cs, xnt[:, :], x1t[:, :], stats, 0, c.const(EPS))
            c.bar()
            c.dma_out(x1_o.rearrange("(i p) d -> p i d", p=128)[:, i], x1t[:, :])
            c.dma_out(xn1_o.rearrange("(i p) d -> p i d", p=128)[:, i], xnt[:, :])
            nc.sync.wait_ge(c.st_sem, c.st_cnt)
            c.bar()
        c.finish()
    return nc



def bld_A10():
    nc = new_nc()
    xn1T = ld(nc, "xn1T", (D, TOK))
    ipwT = ld(nc, "ipwT", (D, 3072))
    ipb_row = ld(nc, "ipb_row", (1, 3072))
    ctx_o = st(nc, "ctx", (TOK, D))
    with (
        nc.sbuf_tensor("xb", [128, 8 * TOK], BF16) as xb,
        nc.sbuf_tensor("wb", [128, 8 * 3072], BF16) as wb,
        nc.sbuf_tensor("ipb", [1, 3072], BF16) as ipb,
        nc.sbuf_tensor("onesb", [1, 512], BF16) as onesb,
        nc.sbuf_tensor("mx0", [128, 3072], F32) as mx0,
        nc.sbuf_tensor("mx1", [128, 3072], F32) as mx1,
        nc.sbuf_tensor("mx2", [128, 3072], F32) as mx2,
        nc.sbuf_tensor("mx3", [128, 3072], F32) as mx3,
        nc.sbuf_tensor("sctmp", [128, 1024], F32) as sctmp,
        nc.sbuf_tensor("scs", [128, 256], F32) as scs,
        nc.sbuf_tensor("mxs", [128, 64], F32) as mxs,
        nc.sbuf_tensor("ctxs", [128, 4 * D], F32) as ctxs,
        nc.psum_tensor("pa", [128, 512], F32) as pa,
        nc.psum_tensor("pb", [128, 512], F32) as pb,
    ):
        c = Ctx(nc)
        c.init()
        cv = c.chained(nc.vector, "cv")
        cs = c.chained(nc.scalar, "cs")
        pe = c.sem("pe")
        pec = [0]

        def pemark(i):
            i.then_inc(pe, 1)
            pec[0] += 1

        c.dma_in(xb[:, :].rearrange("p (c t) -> p c t", c=8),
                 xn1T.rearrange("(c p) t -> p c t", p=128), cast=True)
        c.dma_in(wb[:, :].rearrange("p (c z) -> p c z", c=8),
                 ipwT.rearrange("(c p) z -> p c z", p=128), cast=True)
        c.dma_in(ipb[:, :], ipb_row, cast=True)
        cv.memset(onesb[:, :], 1.0)
        c.wait_in()
        xv = xb[:, :].rearrange("p (c t) -> p c t", c=8)
        wv = wb[:, :].rearrange("p (c z) -> p c z", c=8)
        mixed = [mx0, mx1, mx2, mx3]
        for j in range(4):
            for b in range(4):
                tcols = b * 512 + j * 128
                for zq in range(6):
                    for dc in range(8):
                        nc.tensor.matmul(pa[:, :], xv[:, dc, tcols:tcols + 128],
                                         wv[:, dc, zq * 512:(zq + 1) * 512],
                                         start=(dc == 0), stop=False)
                    ii = nc.tensor.matmul(pa[:, :], onesb[:1, 0:128],
                                          ipb[:1, zq * 512:(zq + 1) * 512],
                                          start=False, stop=True)
                    pemark(ii)
                    nc.vector.wait_ge(pe, pec[0])
                    cv.tensor_copy(mixed[b][:, zq * 512:(zq + 1) * 512], pa[:, :])
                    nc.tensor.wait_ge(cv.sem, cv.n)
            c.bar()
            for i in range(4):
                qv = mixed[i][:, :].rearrange("p (h z) -> p h z", h=16)
                for jj in range(4):
                    kv = mixed[jj][:, :].rearrange("p (h z) -> p h z", h=16)
                    cv.tensor_tensor(sctmp[:, :].rearrange("p (h z) -> p h z", h=16),
                                     qv[:, :, 0:64], kv[:, :, 64:128], ALU.mult)
                    cv.reduce_sum(
                        bass.AP(scs, i * 64 + jj, [[256, 128], [4, 16]]),
                        sctmp[:, :].rearrange("p (h z) -> p h z", h=16),
                        axis=mybir.AxisListType.X)
            c.bar()
            scv = scs[:, :].rearrange("p (i h j) -> p i h j", i=4, h=16)
            mxv = mxs[:, :].rearrange("p (i h) -> p i h", i=4)
            for i in range(4):
                cv.reduce_max(mxv[:, i, :], scv[:, i, :, :], axis=mybir.AxisListType.X)
            c.bar()
            for i in range(4):
                for h in range(16):
                    cv.tensor_scalar(scv[:, i, h, :], scv[:, i, h, :],
                                     mxv[:, i, h:h + 1], 0.125, ALU.subtract, ALU.mult)
            c.bar()
            cs.activation(scs[:, :], scs[:, :], AF.Exp)
            c.bar()
            for i in range(4):
                cv.reduce_sum(mxv[:, i, :], scv[:, i, :, :], axis=mybir.AxisListType.X)
            cv.reciprocal(mxs[:, :], mxs[:, :])
            c.bar()
            cv.memset(ctxs[:, :], 0.0)
            ctv = ctxs[:, :].rearrange("p (i d) -> p i d", i=4)
            for i in range(4):
                for jj in range(4):
                    vv = mixed[jj][:, :].rearrange("p (h z) -> p h z", h=16)
                    for h in range(16):
                        cv.scalar_tensor_tensor(
                            ctv[:, i, h * 64:(h + 1) * 64],
                            vv[:, h, 128:192],
                            bass.AP(scs, i * 64 + h * 4 + jj, [[256, 128], [1, 1]]),
                            ctv[:, i, h * 64:(h + 1) * 64], ALU.mult, ALU.add)
                for h in range(16):
                    cv.tensor_scalar_mul(ctv[:, i, h * 64:(h + 1) * 64],
                                         ctv[:, i, h * 64:(h + 1) * 64],
                                         mxv[:, i, h:h + 1])
            c.bar()
            for b in range(4):
                c.dma_out(ctx_o.rearrange("(i p) d -> p i d", p=128)[:, b * 4 + j],
                          ctv[:, b, :])
            nc.sync.wait_ge(c.st_sem, c.st_cnt)
            c.bar()
        c.finish()
    return nc



def bld_A11():
    nc = new_nc()
    ctxT = ld(nc, "ctxT", (D, TOK))
    opwT = ld(nc, "opwT", (D, D))
    opb_row = ld(nc, "opb_row", (1, D))
    xn1 = ld(nc, "xn1", (TOK, D))
    ng_row = ld(nc, "ng_row", (1, D))
    nb_row = ld(nc, "nb_row", (1, D))
    semb = ld(nc, "semb", (1024, D))
    sg_row = ld(nc, "sg_row", (1, D))
    sb_row = ld(nc, "sb_row", (1, D))
    scal = ld(nc, "scal", (1, 1))
    s1T_o = st(nc, "s1T", (D, TOK))
    xln_o = st(nc, "xln", (TOK, D))
    se_o = st(nc, "se", (1024, D))
    from contextlib import ExitStack
    with ExitStack() as _es:
        def _sb(nm, shp, dt=F32):
            return _es.enter_context(nc.sbuf_tensor(nm, shp, dt))
        ctb = _sb("ctb", [128, 8 * TOK], BF16)
        opb = _sb("opb", [128, 8 * D], BF16)
        opbr = _sb("opbr", [1, D], BF16)
        onesb = _sb("onesb", [1, 512], BF16)
        ones1 = _sb("ones1", [1, 512], F32)
        xt = _sb("xt", [128, D], F32)
        ot = _sb("ot", [128, D], F32)
        stats = _sb("stats", [128, 8], F32)
        cst = _sb("cst", [128, 2], F32)
        gbc = _sb("gbc", [128, D], F32)
        bbc = _sb("bbc", [128, D], F32)
        sgbc = _sb("sgbc", [128, D], F32)
        sbbc = _sb("sbbc", [128, D], F32)
        scbc = _sb("scbc", [128, 1], F32)
        grow = _sb("grow", [1, 4 * D + 1], F32)
        s1t = _sb("s1t", [128, TOK], F32)
        pa = _es.enter_context(nc.psum_tensor("pa", [128, 512], F32))
        c = Ctx(nc)
        c.init()
        cv = c.chained(nc.vector, "cv")
        cs = c.chained(nc.scalar, "cs")
        pe = c.sem("pe")
        pec = [0]

        def pemark(i):
            i.then_inc(pe, 1)
            pec[0] += 1

        c.dma_in(ctb[:, :].rearrange("p (c t) -> p c t", c=8),
                 ctxT.rearrange("(c p) t -> p c t", p=128), cast=True)
        c.dma_in(opb[:, :].rearrange("p (c d) -> p c d", c=8),
                 opwT.rearrange("(c p) d -> p c d", p=128), cast=True)
        c.dma_in(opbr[:, :], opb_row, cast=True)
        c.dma_in(grow[:, 0:D], ng_row)
        c.dma_in(grow[:, D:2 * D], nb_row)
        c.dma_in(grow[:, 2 * D:3 * D], sg_row)
        c.dma_in(grow[:, 3 * D:4 * D], sb_row)
        c.dma_in(grow[:, 4 * D:4 * D + 1], scal)
        cv.memset(onesb[:, :], 1.0)
        cv.memset(ones1[:, :], 1.0)
        c.setup_consts(cst, [EPS])
        c.wait_in()
        for dst, off in ((gbc, 0), (bbc, D), (sgbc, 2 * D), (sbbc, 3 * D)):
            for half in range(2):
                ii = nc.tensor.matmul(pa[:, :], ones1[:1, 0:128],
                                      grow[:, off + half * 512:off + (half + 1) * 512],
                                      start=True, stop=True)
                pemark(ii)
                nc.vector.wait_ge(pe, pec[0])
                cv.tensor_copy(dst[:, half * 512:(half + 1) * 512], pa[:, :])
                nc.tensor.wait_ge(cv.sem, cv.n)
        ii = nc.tensor.matmul(pa[:, 0:1], ones1[:1, 0:128], grow[:, 4 * D:4 * D + 1],
                              start=True, stop=True)
        pemark(ii)
        nc.vector.wait_ge(pe, pec[0])
        cv.tensor_copy(scbc[:, :], pa[:, 0:1])
        nc.tensor.wait_ge(cv.sem, cv.n)
        c.bar()
        cv2 = ctb[:, :].rearrange("p (c t) -> p c t", c=8)
        ov = opb[:, :].rearrange("p (c d) -> p c d", c=8)
        for dt in range(8):
            for q in range(4):
                for kc in range(8):
                    nc.tensor.matmul(pa[:, :], ov[:, kc, dt * 128:(dt + 1) * 128],
                                     cv2[:, kc, q * 512:(q + 1) * 512],
                                     start=(kc == 0), stop=False)
                ii = nc.tensor.matmul(pa[:, :], opbr[:1, dt * 128:(dt + 1) * 128],
                                      onesb[:1, :], start=False, stop=True)
                pemark(ii)
                nc.vector.wait_ge(pe, pec[0])
                cv.tensor_copy(s1t[:, q * 512:(q + 1) * 512], pa[:, :])
                nc.tensor.wait_ge(cv.sem, cv.n)
            c.bar()
            c.dma_out(s1T_o.rearrange("(c p) t -> p c t", p=128)[:, dt], s1t[:, :])
            nc.sync.wait_ge(c.st_sem, c.st_cnt)
            c.bar()
        for i in range(16):
            c.dma_in(xt[:, :], xn1.rearrange("(i p) d -> p i d", p=128)[:, i])
            nc.sync.wait_ge(c.ld_sem, c.ld_cnt)
            c.bar()
            ln_tile(nc, cv, cs, ot[:, :], xt[:, :], stats, 0, c.const(EPS),
                    gb=(gbc[:, :], bbc[:, :]))
            c.bar()
            c.dma_out(xln_o.rearrange("(i p) d -> p i d", p=128)[:, i], ot[:, :])
            nc.sync.wait_ge(c.st_sem, c.st_cnt)
            c.bar()
        for i in range(8):
            c.dma_in(xt[:, :], semb.rearrange("(i p) d -> p i d", p=128)[:, i])
            nc.sync.wait_ge(c.ld_sem, c.ld_cnt)
            c.bar()
            ln_tile(nc, cv, cs, ot[:, :], xt[:, :], stats, 0, c.const(EPS),
                    gb=(sgbc[:, :], sbbc[:, :]))
            cv.tensor_scalar_mul(ot[:, :], ot[:, :], scbc[:, :])
            c.bar()
            c.dma_out(se_o.rearrange("(i p) d -> p i d", p=128)[:, i], ot[:, :])
            nc.sync.wait_ge(c.st_sem, c.st_cnt)
            c.bar()
        c.finish()
    return nc



def bld_A12a():
    nc = new_nc()
    xlnT = ld(nc, "xlnT", (D, TOK))
    seT = ld(nc, "seT", (D, 1024))
    lmax_o = st(nc, "lmax", (128, 32))
    with (
        nc.sbuf_tensor("xb", [128, 8 * TOK], BF16) as xb,
        nc.sbuf_tensor("sb", [128, 8 * 1024], BF16) as sbb,
        nc.sbuf_tensor("lm", [128, 32], F32) as lm,
        nc.psum_tensor("pa", [128, 512], F32) as pa,
        nc.psum_tensor("pb", [128, 512], F32) as pb,
    ):
        c = Ctx(nc)
        c.init()
        cv = c.chained(nc.vector, "cv")
        pe = c.sem("pe")
        pec = [0]
        c.dma_in(xb[:, :].rearrange("p (c t) -> p c t", c=8),
                 xlnT.rearrange("(c p) t -> p c t", p=128), cast=True)
        c.dma_in(sbb[:, :].rearrange("p (c s) -> p c s", c=8),
                 seT.rearrange("(c p) s -> p c s", p=128), cast=True)
        c.wait_in()
        xv = xb[:, :].rearrange("p (c t) -> p c t", c=8)
        sv = sbb[:, :].rearrange("p (c s) -> p c s", c=8)
        for stt in range(8):
            for b in range(4):
                ps = (pa, pb)[b % 2]
                for ch in range(8):
                    ii = nc.tensor.matmul(ps[:, :], sv[:, ch, stt * 128:(stt + 1) * 128],
                                          xv[:, ch, b * 512:(b + 1) * 512],
                                          start=(ch == 0), stop=(ch == 7))
                ii.then_inc(pe, 1)
                pec[0] += 1
                nc.vector.wait_ge(pe, pec[0])
                cv.reduce_max(lm[:, stt * 4 + b:stt * 4 + b + 1], ps[:, :],
                              axis=mybir.AxisListType.X)
                nc.tensor.wait_ge(cv.sem, cv.n)
        c.bar()
        c.dma_out(lmax_o, lm[:, :])
        c.finish()
    return nc



def bld_A12b():
    nc = new_nc()
    lmax_all = ld(nc, "lmax_all", (128, 8 * 32))
    gmax_o = st(nc, "gmax", (128, 32))
    with (
        nc.sbuf_tensor("lm", [128, 8 * 32], F32) as lm,
        nc.sbuf_tensor("gm", [128, 32], F32) as gm,
    ):
        c = Ctx(nc)
        c.init()
        cv = c.chained(nc.vector, "cv")
        c.dma_in(lm[:, :], lmax_all)
        c.wait_in()
        cv.tensor_copy(gm[:, :], lm[:, 0:32])
        for r in range(1, 8):
            cv.tensor_max(gm[:, :], gm[:, :], lm[:, r * 32:(r + 1) * 32])
        c.bar()
        c.dma_out(gmax_o, gm[:, :])
        c.finish()
    return nc



def bld_A12c():
    nc = new_nc()
    xlnT = ld(nc, "xlnT", (D, TOK))
    seT = ld(nc, "seT", (D, 1024))
    gmr = [ld(nc, "gmr%d" % b, (1, 1024)) for b in range(4)]
    etok_o = st(nc, "etok", (TOK, 1024), BF16)
    lsums_o = st(nc, "lsums", (1, 4 * 1024))
    comb_o = st(nc, "comb", (TOK, 1024), BF16)
    from contextlib import ExitStack
    with ExitStack() as _es:
        def _sb(nm, shp, dt=F32):
            return _es.enter_context(nc.sbuf_tensor(nm, shp, dt))
        xb = _sb("xb", [128, 8 * TOK], BF16)
        sbb = _sb("sbb", [128, 8 * 1024], BF16)
        gmrs = _sb("gmrs", [1, 4 * 1024], F32)
        gbc = _sb("gbc", [128, 4 * 1024], F32)
        ones1 = _sb("ones1", [1, 512], F32)
        onesc = _sb("onesc", [128, 1], BF16)
        lgt = _sb("lgt", [128, 1024], F32)
        et = _sb("et", [128, 1024], BF16)
        ect = _sb("ect", [128, 1024], F32)
        cmt = _sb("cmt", [128, 1024], BF16)
        lsums = _sb("lsumss", [1, 4 * 1024], F32)
        stats = _sb("stats", [128, 8], F32)
        pa = _es.enter_context(nc.psum_tensor("pa", [128, 512], F32))
        pr = _es.enter_context(nc.psum_tensor("pr", [1, 512], F32))
        c = Ctx(nc)
        c.init()
        cv = c.chained(nc.vector, "cv")
        cs = c.chained(nc.scalar, "cs")
        pe = c.sem("pe")
        pec = [0]

        def pemark(i):
            i.then_inc(pe, 1)
            pec[0] += 1

        c.dma_in(xb[:, :].rearrange("p (c t) -> p c t", c=8),
                 xlnT.rearrange("(c p) t -> p c t", p=128), cast=True)
        c.dma_in(sbb[:, :].rearrange("p (c s) -> p c s", c=8),
                 seT.rearrange("(c p) s -> p c s", p=128), cast=True)
        for b in range(4):
            c.dma_in(gmrs[:, b * 1024:(b + 1) * 1024], gmr[b])
        cv.memset(onesc[:, :], 1.0)
        cv.memset(ones1[:, :], 1.0)
        cv.memset(lsums[:, :], 0.0)
        c.wait_in()
        for b in range(4):
            for half in range(2):
                ii = nc.tensor.matmul(pa[:, :], ones1[:1, 0:128],
                                      gmrs[:, b * 1024 + half * 512:b * 1024 + (half + 1) * 512],
                                      start=True, stop=True)
                pemark(ii)
                nc.vector.wait_ge(pe, pec[0])
                cv.tensor_copy(gbc[:, b * 1024 + half * 512:b * 1024 + (half + 1) * 512],
                               pa[:, :])
                nc.tensor.wait_ge(cv.sem, cv.n)
        c.bar()
        xv = xb[:, :].rearrange("p (c t) -> p c t", c=8)
        sv = sbb[:, :].rearrange("p (c s) -> p c s", c=8)
        for i in range(16):
            b = i // 4
            for half in range(2):
                for ch in range(8):
                    ii = nc.tensor.matmul(pa[:, :], xv[:, ch, i * 128:(i + 1) * 128],
                                          sv[:, ch, half * 512:(half + 1) * 512],
                                          start=(ch == 0), stop=(ch == 7))
                pemark(ii)
                nc.vector.wait_ge(pe, pec[0])
                cv.tensor_copy(lgt[:, half * 512:(half + 1) * 512], pa[:, :])
                nc.tensor.wait_ge(cv.sem, cv.n)
            c.bar()
            cv.tensor_tensor(ect[:, :], lgt[:, :],
                             gbc[:, b * 1024:(b + 1) * 1024], ALU.subtract)
            c.bar()
            cs.activation(et[:, :], ect[:, :], AF.Exp)
            cv.reduce_max(stats[:, 0:1], lgt[:, :], axis=mybir.AxisListType.X)
            cv.tensor_scalar_mul(stats[:, 1:2], stats[:, 0:1], -1.0)
            c.bar()
            cs.activation(ect[:, :], lgt[:, :], AF.Exp, bias=stats[:, 1:2])
            c.bar()
            cv.reduce_sum(stats[:, 2:3], ect[:, :], axis=mybir.AxisListType.X)
            cv.reciprocal(stats[:, 3:4], stats[:, 2:3])
            cv.tensor_scalar_mul(cmt[:, :], ect[:, :], stats[:, 3:4])
            c.bar()
            for half in range(2):
                ii = nc.tensor.matmul(pr[:, :], onesc[:, :],
                                      et[:, half * 512:(half + 1) * 512],
                                      start=True, stop=True)
                pemark(ii)
                nc.vector.wait_ge(pe, pec[0])
                cv.tensor_add(lsums[:, b * 1024 + half * 512:b * 1024 + (half + 1) * 512],
                              lsums[:, b * 1024 + half * 512:b * 1024 + (half + 1) * 512],
                              pr[:1, :])
                nc.tensor.wait_ge(cv.sem, cv.n)
            c.bar()
            c.dma_out(etok_o.rearrange("(i p) s -> p i s", p=128)[:, i], et[:, :])
            c.dma_out(comb_o.rearrange("(i p) s -> p i s", p=128)[:, i], cmt[:, :])
            nc.sync.wait_ge(c.st_sem, c.st_cnt)
            c.bar()
        c.dma_out(lsums_o, lsums[:, :])
        c.finish()
    return nc



def bld_A12d():
    nc = new_nc()
    srows = [ld(nc, "sr%d" % r, (1, 4096)) for r in range(8)]
    gs_o = st(nc, "gsums", (1, 4096))
    with (
        nc.sbuf_tensor("acc", [1, 4096], F32) as acc,
        nc.sbuf_tensor("row", [1, 4096], F32) as row,
    ):
        c = Ctx(nc)
        c.init()
        cv = c.chained(nc.vector, "cv")
        c.dma_in(acc[:, :], srows[0])
        c.wait_in()
        for r in range(1, 8):
            c.dma_in(row[:, :], srows[r])
            nc.sync.wait_ge(c.ld_sem, c.ld_cnt)
            c.bar()
            cv.tensor_add(acc[:, :], acc[:, :], row[:, :])
            c.bar()
        c.dma_out(gs_o, acc[:, :])
        c.finish()
    return nc



def bld_A13():
    nc = new_nc()
    etok = ld(nc, "etok", (TOK, 1024), BF16)
    xln = ld(nc, "xln", (TOK, D))
    part_o = st(nc, "part", (32 * 128, D))
    with (
        nc.sbuf_tensor("eb", [128, 16 * 1024], BF16) as eb,
        nc.sbuf_tensor("xb", [128, 16 * D], BF16) as xb,
        nc.sbuf_tensor("ob", [128, D], F32) as ob,
        nc.psum_tensor("pa", [128, 512], F32) as pa,
    ):
        c = Ctx(nc)
        c.init()
        cv = c.chained(nc.vector, "cv")
        pe = c.sem("pe")
        pec = [0]
        c.dma_in(eb[:, :].rearrange("p (i s) -> p i s", i=16),
                 etok.rearrange("(i p) s -> p i s", p=128))
        c.dma_in(xb[:, :].rearrange("p (i d) -> p i d", i=16),
                 xln.rearrange("(i p) d -> p i d", p=128), cast=True)
        c.wait_in()
        ev = eb[:, :].rearrange("p (i s) -> p i s", i=16)
        xv = xb[:, :].rearrange("p (i d) -> p i d", i=16)
        for b in range(4):
            for blk in range(8):
                for q in range(2):
                    for ti in range(4):
                        ii = nc.tensor.matmul(pa[:, :], ev[:, b * 4 + ti, blk * 128:(blk + 1) * 128],
                                              xv[:, b * 4 + ti, q * 512:(q + 1) * 512],
                                              start=(ti == 0), stop=(ti == 3))
                    ii.then_inc(pe, 1)
                    pec[0] += 1
                    nc.vector.wait_ge(pe, pec[0])
                    cv.tensor_copy(ob[:, q * 512:(q + 1) * 512], pa[:, :])
                    nc.tensor.wait_ge(cv.sem, cv.n)
                c.bar()
                c.dma_out(part_o.rearrange("(x p) d -> p x d", p=128)[:, blk * 4 + b], ob[:, :])
                nc.sync.wait_ge(c.st_sem, c.st_cnt)
                c.bar()
        c.finish()
    return nc



def bld_A14():
    nc = new_nc()
    part8 = ld(nc, "part8", (8 * 512, D))
    Sc = ld(nc, "Sc", (128, 4))
    slots_o = st(nc, "slots", (128, 4 * D))
    with (
        nc.sbuf_tensor("acc", [128, 4 * D], F32) as acc,
        nc.sbuf_tensor("buf", [128, 4 * D], F32) as buf,
        nc.sbuf_tensor("sc", [128, 4], F32) as scb,
        nc.sbuf_tensor("rs", [128, 4], F32) as rsb,
    ):
        c = Ctx(nc)
        c.init()
        cv = c.chained(nc.vector, "cv")
        pv = part8.rearrange("(r b p) d -> p r b d", r=8, p=128)
        c.dma_in(acc[:, :].rearrange("p (b d) -> p b d", b=4), pv[:, 0])
        c.dma_in(scb[:, :], Sc)
        c.wait_in()
        for r in range(1, 8):
            c.dma_in(buf[:, :].rearrange("p (b d) -> p b d", b=4), pv[:, r])
            nc.sync.wait_ge(c.ld_sem, c.ld_cnt)
            c.bar()
            cv.tensor_add(acc[:, :], acc[:, :], buf[:, :])
            c.bar()
        cv.reciprocal(rsb[:, :], scb[:, :])
        av = acc[:, :].rearrange("p (b d) -> p b d", b=4)
        for b in range(4):
            cv.tensor_scalar_mul(av[:, b, :], av[:, b, :], rsb[:, b:b + 1])
        c.bar()
        c.dma_out(slots_o, acc[:, :])
        c.finish()
    return nc



def bld_A15():
    nc = new_nc()
    slotsT = ld(nc, "slotsT", (4 * 1024, 128))
    Wg = ld(nc, "Wg", (2 * D, 2048))
    Wu = ld(nc, "Wu", (2 * D, 2048))
    Wd = ld(nc, "Wd", (2 * 2048, D))
    bg_row = ld(nc, "bg_row", (1, 2 * 2048))
    bu_row = ld(nc, "bu_row", (1, 2 * 2048))
    bd_row = ld(nc, "bd_row", (1, 2 * D))
    eye = ld(nc, "eye", (128, 128))
    eout_o = st(nc, "eout", (128, 4 * D), BF16)
    from contextlib import ExitStack
    with ExitStack() as _es:
        def _sb(nm, shp, dt=F32):
            return _es.enter_context(nc.sbuf_tensor(nm, shp, dt))
        slb = _sb("slb", [128, 4 * 8 * 128], BF16)
        eyeb = _sb("eyeb", [128, 128], BF16)
        w0 = _sb("w0", [128, 2048], BF16)
        w1 = _sb("w1", [128, 2048], BF16)
        bgr = _sb("bgr", [1, 2 * 2048], BF16)
        bur = _sb("bur", [1, 2 * 2048], BF16)
        bdr = _sb("bdr", [1, 2 * D], BF16)
        ones64 = _sb("ones64", [1, 64], BF16)
        gate = _sb("gate", [64, 2048], BF16)
        up = _sb("up", [64, 2048], BF16)
        gu = _sb("gu", [64, 2048], BF16)
        guT = _sb("guT", [128, 16 * 64], BF16)
        eout = _sb("eouts", [128, 4 * D], BF16)
        pa = _es.enter_context(nc.psum_tensor("pa", [64, 512], F32))
        pt = _es.enter_context(nc.psum_tensor("pt", [128, 64], BF16))
        c = Ctx(nc)
        c.init()
        cv = c.chained(nc.vector, "cv")
        cs = c.chained(nc.scalar, "cs")
        pe = c.sem("pe")
        pec = [0]

        def pemark(i):
            i.then_inc(pe, 1)
            pec[0] += 1

        c.dma_in(slb[:, :].rearrange("p (b c s) -> p b c s", b=4, c=8),
                 slotsT.rearrange("(b c p) s -> p b c s", b=4, p=128), cast=True)
        c.dma_in(eyeb[:, :], eye, cast=True)
        c.dma_in(bgr[:, :], bg_row, cast=True)
        c.dma_in(bur[:, :], bu_row, cast=True)
        c.dma_in(bdr[:, :], bd_row, cast=True)
        cv.memset(ones64[:, :], 1.0)
        c.wait_in()
        slv = slb[:, :].rearrange("p (b c s) -> p b c s", b=4, c=8)
        wgv = Wg.rearrange("(e c p) f -> p e c f", e=2, p=128)
        wuv = Wu.rearrange("(e c p) f -> p e c f", e=2, p=128)
        wdv = Wd.rearrange("(e f p) d -> p e f d", e=2, p=128)
        guv = guT[:, :].rearrange("p (f s) -> p f s", f=16)
        eov = eout[:, :].rearrange("p (b d) -> p b d", b=4)
        pq = [_es.enter_context(nc.psum_tensor("pq%d" % q, [64, 512], F32)) for q in range(4)]
        for el in range(2):
            for b in range(4):
                for dst, wv, brow in ((gate, wgv, bgr), (up, wuv, bur)):
                    for dc in range(8):
                        wbuf = (w0, w1)[dc % 2]
                        nc.gpsimd.wait_ge(pe, pec[0])
                        c.dma_in(wbuf[:, :], wv[:, el, dc, :], cast=True)
                        nc.sync.wait_ge(c.ld_sem, c.ld_cnt)
                        nc.gpsimd.wait_ge(c.ld_sem, c.ld_cnt)
                        c.bar()
                        for q in range(4):
                            nc.tensor.matmul(pq[q][:, :], slv[:, b, dc, el * 64:(el + 1) * 64],
                                             wbuf[:, q * 512:(q + 1) * 512],
                                             start=(dc == 0), stop=False)
                    for q in range(4):
                        ii = nc.tensor.matmul(pq[q][:, :], ones64[:1, :],
                                              brow[:1, el * 2048 + q * 512:el * 2048 + (q + 1) * 512],
                                              start=False, stop=True)
                        pemark(ii)
                        nc.scalar.wait_ge(pe, pec[0])
                        nc.scalar.wait_ge(cv.sem, cv.n)
                        fn = AF.Gelu if dst is gate else AF.Identity
                        cs.activation(dst[0:64, q * 512:(q + 1) * 512], pq[q][0:64, :], fn)
                    nc.tensor.wait_ge(cs.sem, cs.n)
                    c.bar()
                cv.tensor_mul(gu[0:64, :], gate[0:64, :], up[0:64, :])
                c.bar()
                for fc in range(16):
                    ii = nc.tensor.transpose(pt[:, :], gu[0:64, fc * 128:(fc + 1) * 128],
                                             eyeb[0:64, 0:64])
                    pemark(ii)
                    nc.vector.wait_ge(pe, pec[0])
                    cv.tensor_copy(guv[:, fc, :], pt[:, 0:64])
                    nc.tensor.wait_ge(cv.sem, cv.n)
                c.bar()
                for fc in range(16):
                    wbuf = (w0, w1)[fc % 2]
                    nc.gpsimd.wait_ge(pe, pec[0])
                    c.dma_in(wbuf[:, 0:1024], wdv[:, el, fc, :], cast=True)
                    nc.sync.wait_ge(c.ld_sem, c.ld_cnt)
                    nc.gpsimd.wait_ge(c.ld_sem, c.ld_cnt)
                    c.bar()
                    for q in range(2):
                        nc.tensor.matmul(pq[q][:, :], guv[:, fc, :],
                                         wbuf[:, q * 512:q * 512 + 512],
                                         start=(fc == 0), stop=False)
                for q in range(2):
                    ii = nc.tensor.matmul(pq[q][:, :], ones64[:1, :],
                                          bdr[:1, el * D + q * 512:el * D + (q + 1) * 512],
                                          start=False, stop=True)
                    pemark(ii)
                    nc.vector.wait_ge(pe, pec[0])
                    cv.tensor_copy(eout[el * 64:(el + 1) * 64, b * D + q * 512:b * D + (q + 1) * 512],
                                   pq[q][0:64, :])
                nc.tensor.wait_ge(cv.sem, cv.n)
                c.bar()
        c.dma_out(eout_o, eout[:, :])
        c.finish()
    return nc



def bld_A16():
    nc = new_nc()
    eoutA = ld(nc, "eoutA", (1024, 4 * D), BF16)
    combT = ld(nc, "combT", (1024, TOK), BF16)
    s1T = ld(nc, "s1T", (D, TOK))
    x1T = ld(nc, "x1T", (D, TOK))
    x2T_o = st(nc, "x2T", (D, TOK))
    from contextlib import ExitStack
    with ExitStack() as _es:
        def _sb(nm, shp, dt=F32):
            return _es.enter_context(nc.sbuf_tensor(nm, shp, dt))
        eo = _sb("eo", [128, 8 * 4 * D], BF16)
        cb = _sb("cb", [128, 8 * TOK], BF16)
        s1t = _sb("s1t", [128, 512], F32)
        x1t = _sb("x1t", [128, 512], F32)
        g1 = _sb("g1", [128, 512], F32)
        pa = _es.enter_context(nc.psum_tensor("pa", [128, 512], F32))
        c = Ctx(nc)
        c.init()
        cv = c.chained(nc.vector, "cv")
        cs = c.chained(nc.scalar, "cs")
        pe = c.sem("pe")
        pec = [0]
        c.dma_in(eo[:, :].rearrange("p (r b d) -> p r b d", r=8, b=4),
                 eoutA.rearrange("(r p) (b d) -> p r b d", p=128, b=4))
        c.dma_in(cb[:, :].rearrange("p (r t) -> p r t", r=8),
                 combT.rearrange("(r p) t -> p r t", p=128))
        c.wait_in()
        eov = eo[:, :].rearrange("p (r b d) -> p r b d", r=8, b=4)
        cbv = cb[:, :].rearrange("p (r t) -> p r t", r=8)
        s1tv = s1T.rearrange("(c p) t -> p c t", p=128)
        x1tv = x1T.rearrange("(c p) t -> p c t", p=128)
        x2tv = x2T_o.rearrange("(c p) t -> p c t", p=128)
        for dt in range(8):
            for b in range(4):
                tsl = slice(b * 512, (b + 1) * 512)
                c.dma_in(s1t[:, :], s1tv[:, dt, tsl])
                c.dma_in(x1t[:, :], x1tv[:, dt, tsl])
                for r in range(8):
                    ii = nc.tensor.matmul(pa[:, :], eov[:, r, b, dt * 128:(dt + 1) * 128],
                                          cbv[:, r, b * 512:(b + 1) * 512],
                                          start=(r == 0), stop=(r == 7))
                ii.then_inc(pe, 1)
                pec[0] += 1
                nc.scalar.wait_ge(pe, pec[0])
                nc.scalar.wait_ge(cv.sem, cv.n)
                cs.activation(g1[:, :], pa[:, :], AF.Silu)
                nc.tensor.wait_ge(cs.sem, cs.n)
                nc.vector.wait_ge(cs.sem, cs.n)
                nc.vector.wait_ge(c.ld_sem, c.ld_cnt)
                cv.tensor_mul(g1[:, :], g1[:, :], s1t[:, :])
                cv.tensor_add(x1t[:, :], x1t[:, :], g1[:, :])
                c.bar()
                c.dma_out(x2tv[:, dt, tsl], x1t[:, :])
                nc.sync.wait_ge(c.st_sem, c.st_cnt)
                c.bar()
        c.finish()
    return nc


def run(name, bld, in_maps, trace=False):
    if name not in _prog_cache:
        _prog_cache[name] = bld()
    nc = _prog_cache[name]
    res = run_bass_kernel_spmd(nc, in_maps, core_ids=list(range(NCORE)))
    return res.results


def _t(a):
    return np.ascontiguousarray(np.asarray(a).T)


def kernel(**inp):
    f = {k: f32c(v) for k, v in inp.items() if k not in ("L1_attn_key", "L1_attn_value")}
    x_full = f["L0_attn_x"].reshape(B, S, D)

    def slab(full, c, width=W):
        return np.ascontiguousarray(full.reshape(B, S, -1)[:, c * W:(c + 1) * W].reshape(B * W, -1))

    rng8 = range(NCORE)
    r1 = run("A1", bld_A1, [{
        "x": slab(x_full, c),
        "g_row": f["L0_attn_norm_g"][None, :], "b_row": f["L0_attn_norm_b"][None, :],
    } for c in rng8])
    r2 = run("A2", bld_A2, [{
        "xn2T": _t(r1[c]["xn2"]), "BwT": _t(f["L0_attn_B_w"]),
        "Bb_row": f["L0_attn_B_b"][None, :], "Alog_c": f["L0_attn_A_log"][:, None],
        "iota_t": np.arange(c * W, (c + 1) * W, dtype=np.float32)[None, :],
        "iota_l": np.arange(W, dtype=np.float32)[None, :],
    } for c in rng8])
    E_all = np.concatenate([r2[c]["E"] for c in rng8], axis=1)
    r3 = run("A3", bld_A3, [{"E_all": E_all, "A512": r2[0]["A512"]} for c in rng8])
    S_all = r3[0]["S_all"]
    fb_row = np.concatenate([f["L0_attn_ff1_b"], f["L0_attn_ff2_b"],
                             f["L0_attn_ta_b"], f["L0_attn_tb_b"]])[None, :]
    r4 = run("A4", bld_A4, [{
        "yloc": r2[c]["yloc"], "aj1": r2[c]["aj1"],
        "carry": S_all[:, (c - 1) * B:c * B] if c > 0 else np.zeros((N, B), np.float32),
        "Cw": f["L0_attn_C_w"], "CbT": chunk128(f["L0_attn_C_b"]),
        "ff1T": _t(f["L0_attn_ff1_w"]), "ff2T": _t(f["L0_attn_ff2_w"]),
        "taT": _t(f["L0_attn_ta_w"]), "tbT": _t(f["L0_attn_tb_w"]), "fb": fb_row,
    } for c in rng8])
    ffT = {"ff1T": _t(f["L0_attn_ff1_w"]), "ff2T": _t(f["L0_attn_ff2_w"]),
           "taT": _t(f["L0_attn_ta_w"]), "tbT": _t(f["L0_attn_tb_w"])}
    r5 = run("A5", bld_A5, [dict(P=r4[c]["P"], **ffT) for c in rng8])
    r6 = run("A6", bld_A6, [dict(
        P=np.ascontiguousarray(
            np.asarray(r4[c]["P"]).reshape(128, 12, B, W)[:, :, :, :BFIX].reshape(128, 12 * B * BFIX)),
        carry=(np.asarray(r5[c - 1]["hend"], np.float32) if c > 0 else np.zeros((128, 16), np.float32)),
        **ffT) for c in rng8])
    cfc = []
    for c in rng8:
        sw = np.asarray(r5[c]["cfc"])
        fx = np.asarray(r6[c]["cfc"])
        cfc.append(np.concatenate([fx, sw[:, BFIX * 16:]], axis=1))
    r7 = run("A7", bld_A7, [{
        "cfc": cfc[c], "owT": _t(f["L0_attn_out_w"]), "ob_row": f["L0_attn_out_b"][None, :],
        "xn": r1[c]["xn"], "xnT": _t(r1[c]["xn"]),
        "rw1": f["L0_ffn_rw1"], "rb1_row": f["L0_ffn_rb1"][None, :],
        "rw2": f["L0_ffn_rw2"], "rb2_row": f["L0_ffn_rb2"][None, :],
    } for c in rng8])
    r8a = run("A8a", bld_A8a, [{"rlt": _t(r7[c]["rlog"])} for c in rng8])
    uc = np.zeros((8, 8 * 128), np.float32)
    for e in range(8):
        uc[e, e * 128:(e + 1) * 128] = 1.0
    ew1f = f["L0_ffn_ew1"].reshape(8 * D, 2048)
    ew2f = f["L0_ffn_ew2"].reshape(8 * 2048, D)
    r8b = run("A8b", bld_A8b, [{
        "wsl": _t(r8a[c]["w"]), "xnT": _t(r1[c]["xn"]),
        "ew1": ew1f, "eb1": f["L0_ffn_eb1"], "ew2": ew2f, "eb2": f["L0_ffn_eb2"], "uc": uc,
    } for c in rng8])
    r9 = run("A9", bld_A9, [{
        "x": slab(x_full, c), "s0": r7[c]["s0"],
        "g0": np.ascontiguousarray(np.asarray(r8b[c]["g0T"]).T),
    } for c in rng8])
    r10 = run("A10", bld_A10, [{
        "xn1T": _t(r9[c]["xn1"]), "ipwT": _t(f["L1_attn_in_proj_w"]),
        "ipb_row": f["L1_attn_in_proj_b"][None, :],
    } for c in rng8])
    r11 = run("A11", bld_A11, [{
        "ctxT": _t(r10[c]["ctx"]), "opwT": _t(f["L1_attn_out_proj_w"]),
        "opb_row": f["L1_attn_out_proj_b"][None, :], "xn1": r9[c]["xn1"],
        "ng_row": f["L1_ffn_norm_g"][None, :], "nb_row": f["L1_ffn_norm_b"][None, :],
        "semb": f["L1_ffn_slot_embeds"].reshape(1024, D),
        "sg_row": f["L1_ffn_slot_g"][None, :], "sb_row": f["L1_ffn_slot_b"][None, :],
        "scal": f["L1_ffn_scale"].reshape(1, 1),
    } for c in rng8])
    seT = _t(r11[0]["se"])
    r12a = run("A12a", bld_A12a, [{"xlnT": _t(r11[c]["xln"]), "seT": seT} for c in rng8])
    lmax_all = np.concatenate([r12a[c]["lmax"] for c in rng8], axis=1)
    r12b = run("A12b", bld_A12b, [{"lmax_all": lmax_all} for c in rng8])
    gmax = r12b[0]["gmax"]
    gmr = np.ascontiguousarray(gmax.reshape(128, 8, 4).transpose(2, 1, 0).reshape(4, 1024))
    r12c = run("A12c", bld_A12c, [dict(
        {"xlnT": _t(r11[c]["xln"]), "seT": seT},
        **{"gmr%d" % b: gmr[b][None, :] for b in range(4)}) for c in rng8])
    r12d = run("A12d", bld_A12d, [{("sr%d" % r): r12c[r]["lsums"] for r in rng8} for c in rng8])
    gsums = r12d[0]["gsums"].reshape(4, 1024)
    r13 = run("A13", bld_A13, [{"etok": r12c[c]["etok"], "xln": r11[c]["xln"]} for c in rng8])
    r14in = []
    for c in rng8:
        p8 = np.concatenate([r13[r]["part"][c * 512:(c + 1) * 512] for r in rng8], axis=0)
        Sc = np.ascontiguousarray(gsums[:, c * 128:(c + 1) * 128].T)
        r14in.append({"part8": p8, "Sc": Sc})
    r14 = run("A14", bld_A14, r14in)
    r15in = []
    for c in rng8:
        sl = np.stack([_t(r14[c]["slots"][:, b * D:(b + 1) * D]) for b in range(B)])
        r15in.append({
            "slotsT": np.ascontiguousarray(sl.reshape(4 * D, 128)),
            "Wg": f["L1_ffn_Wg"][2 * c:2 * c + 2].reshape(2 * D, 2048),
            "Wu": f["L1_ffn_Wu"][2 * c:2 * c + 2].reshape(2 * D, 2048),
            "Wd": f["L1_ffn_Wd"][2 * c:2 * c + 2].reshape(2 * 2048, D),
            "bg_row": f["L1_ffn_bg"][2 * c:2 * c + 2].reshape(1, 2 * 2048),
            "bu_row": f["L1_ffn_bu"][2 * c:2 * c + 2].reshape(1, 2 * 2048),
            "bd_row": f["L1_ffn_bd"][2 * c:2 * c + 2].reshape(1, 2 * D),
            "eye": np.eye(128, dtype=np.float32),
        })
    r15 = run("A15", bld_A15, r15in)
    eoutA = np.concatenate([np.asarray(r15[c]["eout"]) for c in rng8], axis=0)
    r16 = run("A16", bld_A16, [{
        "eoutA": eoutA,
        "combT": np.ascontiguousarray(np.asarray(r12c[c]["comb"]).T),
        "s1T": r11[c]["s1T"], "x1T": _t(r9[c]["x1"]),
    } for c in rng8])
    out = np.zeros((B, S, D), np.float32)
    for c in rng8:
        out[:, c * W:(c + 1) * W, :] = r16[c]["x2T"].T.reshape(B, W, D)
    return out
